# version 01
# baseline (speedup 1.0000x reference)
import math
import numpy as np
import ml_dtypes
import concourse.bass as bass
import concourse.mybir as mybir
from concourse.bass_utils import run_bass_kernel_spmd

F32 = mybir.dt.float32
BF16 = mybir.dt.bfloat16
U8 = mybir.dt.uint8
AF = mybir.ActivationFunctionType
ALU = mybir.AluOpType
AX = mybir.AxisListType
BF = ml_dtypes.bfloat16

NCORE = 8
B_GLOB, T, D = 16, 2048, 2048
B = B_GLOB // NCORE
NT = B * T
DT = D // 128
H, DK, DV, M = 16, 128, 128, 128
DH0 = 8192
DH1 = 5632
C = 128
NCH = T // C
ALPHA = -0.68
SCALE0 = 1.0 / math.sqrt(128.0)
SCALE1 = 128.0 ** (-0.5)
LN_EPS = 1e-5
L2_EPS = 1e-6
RMS_EPS = 1e-5
BIT_EPS = 1e-8

ENGINES = ("sync", "gpsimd", "tensor", "vector", "scalar")

class KB:
    def __init__(self, nc):
        self.nc = nc
        self.steps = []
        self.count = {e: 0 for e in ENGINES}
        self.watermark = {e: {e2: 0 for e2 in ENGINES} for e in ENGINES}

    def on(self, eng, fn, waits=(), inc=None):
        raise NotImplementedError

    def step(self, eng, fn, waits=(), dma=False, ninc=None):
        inc = ninc if ninc is not None else (16 if dma else 1)
        w = []
        wm = self.watermark[eng]
        for (we, wv) in waits:
            if wv > wm[we]:
                wm[we] = wv
                w.append((we, wv))
        self.steps.append((eng, tuple(w), fn, inc))
        self.count[eng] += inc
        return (eng, self.count[eng])

    def token(self, eng):
        return (eng, self.count[eng])

    def barrier(self):
        return [(e, self.count[e]) for e in ENGINES]

    def finalize(self, block, sems):
        per = {e: [] for e in ENGINES}
        for (eng, waits, fn, inc) in self.steps:
            per[eng].append((waits, fn, inc))
        final = {e: self.count[e] for e in ENGINES}

        def body(name):
            def _(eobj):
                for (waits, fn, inc) in per[name]:
                    for (we, wv) in waits:
                        eobj.wait_ge(sems[we], wv)
                    ins = fn(eobj)
                    if ins is not None:
                        ins.then_inc(sems[name], inc)
                for e2 in ENGINES:
                    if final[e2] > 0:
                        eobj.wait_ge(sems[e2], final[e2])
            return _
        for e in ENGINES:
            if per[e] or True:
                getattr(block, e)(body(e))


def host_prepare(inputs):
    d = {}
    x = np.ascontiguousarray(inputs["L0_attn_query"])
    d["_x_shards"] = [np.ascontiguousarray(x[i*B:(i+1)*B].reshape(NT, D)) for i in range(NCORE)]
    d["w_q"] = np.ascontiguousarray(inputs["L0_attn_w_q"])
    d["w_k"] = np.ascontiguousarray(inputs["L0_attn_w_k"])
    d["w_v"] = np.ascontiguousarray(inputs["L0_attn_w_v"])
    d["w_o"] = np.ascontiguousarray(inputs["L0_attn_w_o"])
    d["b_qkv"] = np.ascontiguousarray(np.stack([
        inputs["L0_attn_b_q"].reshape(16, 128).T,
        inputs["L0_attn_b_k"].reshape(16, 128).T], axis=0))
    d["b_v_row"] = np.ascontiguousarray(inputs["L0_attn_b_v"].reshape(1, D))
    d["b_o_row"] = np.ascontiguousarray(inputs["L0_attn_b_o"].reshape(1, D))
    n = np.arange(DH0)
    qb, r = n // 1024, n % 1024
    j, u = r // 256, r % 256
    perm = 4 * (qb * 256 + u) + j
    d["w_gate"] = np.ascontiguousarray(inputs["L0_ffn_w_gate"][:, perm])
    d["w_up"] = np.ascontiguousarray(inputs["L0_ffn_w_up"][:, perm])
    d["w_down"] = np.ascontiguousarray(inputs["L0_ffn_w_down"][perm, :])
    d["_ffn_perm"] = perm
    for k in ["Wq", "Wk", "Wv", "Ww", "Wf1", "Wf2", "Wb", "Wg1", "Wg2", "Wo"]:
        d["l1_" + k] = np.ascontiguousarray(inputs["L1_attn_" + k])
    cvs = np.stack([inputs["L1_attn_cq"].reshape(16, 128, 4),
                    inputs["L1_attn_ck"].reshape(16, 128, 4),
                    inputs["L1_attn_cv"].reshape(16, 128, 4)], axis=0)
    d["l1_conv"] = np.ascontiguousarray(cvs.transpose(2, 0, 1, 3).reshape(128, 3 * 64))
    d["l1_bg2"] = np.ascontiguousarray(inputs["L1_attn_bg2"].reshape(16, 128).T)
    d["l1_normw"] = np.ascontiguousarray(inputs["L1_attn_norm_w"].reshape(128, 1))
    d["fw1"] = np.ascontiguousarray(inputs["L1_ffn_w1"])
    d["fw2"] = np.ascontiguousarray(inputs["L1_ffn_w2"])
    d["fw3"] = np.ascontiguousarray(inputs["L1_ffn_w3"])
    d["c_ident"] = np.eye(128, dtype=np.float32)
    s = np.arange(C)[:, None]; t = np.arange(C)[None, :]
    gt = np.exp(ALPHA * (t - s)) * (t >= s)
    d["c_gt"] = gt.astype(np.float32)
    tri = (t >= s).astype(np.float32)
    d["c_tri"] = np.ascontiguousarray(tri)
    d["c_pramp"] = np.broadcast_to(np.exp(ALPHA * ((np.arange(2048) % C) + 1))[None, :], (128, 2048)).astype(np.float32).copy()
    d["c_prampend"] = np.broadcast_to(np.exp(ALPHA * (C - 1 - np.arange(C)))[None, :], (128, C)).astype(np.float32).copy()
    d["c_aper"] = np.broadcast_to((ALPHA * ((np.arange(2048) % C) + 1))[None, :], (128, 2048)).astype(np.float32).copy()
    d["c_ones"] = np.ones((128, 2), dtype=np.float32); d["c_ones"][:, 1] = 1.0 / 128.0
    d["c_ones_row"] = np.ones((1, 128), dtype=np.float32)
    dm = np.zeros((128, 4, 4, 128), dtype=np.float32)
    tk = np.arange(128)[:, None]; tq = np.arange(128)[None, :]
    for c in range(4):
        for cc in range(4):
            if cc < c: pass
            elif cc == c: dm[:, c, cc, :] = (tq >= tk)
            else: dm[:, c, cc, :] = 1.0
    d["c_dmask"] = np.ascontiguousarray(dm.reshape(128, 4 * 512))
    return d



class P:
    pass

def alloc_consts(p, stack):
    nc = p.nc
    p.ident = stack.enter_context(nc.sbuf_tensor("sb_c_ident", [128, 128], BF16))
    p.gt = stack.enter_context(nc.sbuf_tensor("sb_c_gt", [128, C], F32))
    p.tri = stack.enter_context(nc.sbuf_tensor("sb_c_tri", [128, C], BF16))
    p.pramp = stack.enter_context(nc.sbuf_tensor("sb_c_pramp", [128, 2048], BF16))
    p.prampend = stack.enter_context(nc.sbuf_tensor("sb_c_prampend", [128, C], BF16))
    p.aper = stack.enter_context(nc.sbuf_tensor("sb_c_aper", [128, 2048], F32))
    p.ones = stack.enter_context(nc.sbuf_tensor("sb_c_ones", [128, 2], BF16))
    p.ones_row = stack.enter_context(nc.sbuf_tensor("sb_c_ones_row", [1, 128], BF16))
    p.zeros128 = stack.enter_context(nc.sbuf_tensor("sb_c_zeros", [128, 4], F32))
    p.cc = stack.enter_context(nc.sbuf_tensor("sb_c_cc", [128, 4], F32))
    k = p.kb
    def ld(dst, src, cast):
        eng = "gpsimd" if cast else "sync"
        return k.step(eng, lambda e, d=dst, s=src: e.dma_start(out=d[:], in_=s), dma=True)
    ld(p.ident, p.io["c_ident"], True)
    ld(p.gt, p.io["c_gt"], False)
    ld(p.tri, p.io["c_tri"], True)
    ld(p.pramp, p.io["c_pramp"], True)
    ld(p.prampend, p.io["c_prampend"], True)
    ld(p.aper, p.io["c_aper"], False)
    ld(p.ones, p.io["c_ones"], True)
    ld(p.ones_row, p.io["c_ones_row"], True)
    k.step("vector", lambda e: e.memset(p.zeros128[:], 0.0),)
    k.step("vector", lambda e: e.memset(p.cc[:, 0:1], 1.0))
    k.step("vector", lambda e: e.memset(p.cc[:, 1:2], ALPHA * C))
    k.step("vector", lambda e: e.memset(p.cc[:, 2:3], RMS_EPS))
    k.step("vector", lambda e: e.memset(p.cc[:, 3:4], L2_EPS))
    p.const_tok = k.barrier()

def phase_A(p):
    nc, k = p.nc, p.kb
    import contextlib
    with contextlib.ExitStack() as st:
        xbuf = st.enter_context(nc.sbuf_tensor("a_x", [128, 2 * D], F32))
        x2 = st.enter_context(nc.sbuf_tensor("a_x2", [128, 2 * D], F32))
        xn = st.enter_context(nc.sbuf_tensor("a_xn", [128, 2 * D], BF16))
        xnT = st.enter_context(nc.sbuf_tensor("a_xnT", [128, 2 * D], BF16))
        stt = st.enter_context(nc.sbuf_tensor("a_st", [128, 8], F32))
        x_hbm = p.io["x"]
        xn0 = p.io["xn0_cm"]
        lastV = [None, None]
        lastT = [None, None]
        lastST = [None, None]
        lastTv2 = [None, None]
        lastTr = [None, None]
        for i in range(NT // 128):
            bi = i % 2
            w = list(p.const_tok) if i == 0 else []
            if lastV[bi] is not None:
                w.append(lastV[bi])
            tl = k.step("gpsimd", lambda e, i=i, bi=bi: e.dma_start(
                out=xbuf[:, bi * D:(bi + 1) * D], in_=x_hbm[i * 128:(i + 1) * 128, :]),
                waits=w, dma=True)
            tv1 = k.step("vector", lambda e, bi=bi: e.reduce_sum(
                stt[:, 0:1], xbuf[:, bi * D:(bi + 1) * D], axis=AX.X), waits=[tl])
            ws1 = [tl] + ([lastTv2[bi]] if lastTv2[bi] else [])
            ts1 = k.step("scalar", lambda e, bi=bi: e.activation(
                x2[:, bi * D:(bi + 1) * D], xbuf[:, bi * D:(bi + 1) * D], AF.Square), waits=ws1)
            tv2 = k.step("vector", lambda e, bi=bi: e.reduce_sum(stt[:, 1:2], x2[:, bi * D:(bi + 1) * D], axis=AX.X), waits=[ts1])
            lastTv2[bi] = tv2
            ta = k.step("vector", lambda e: e.tensor_scalar(stt[:, 2:3], stt[:, 0:1], 1.0 / D, None, ALU.mult), waits=[tv1])
            tb = k.step("vector", lambda e: e.tensor_scalar(stt[:, 3:4], stt[:, 1:2], 1.0 / D, None, ALU.mult), waits=[tv2])
            tc = k.step("vector", lambda e: e.tensor_tensor(stt[:, 4:5], stt[:, 2:3], stt[:, 2:3], ALU.mult), waits=[ta])
            tv3 = k.step("vector", lambda e: e.scalar_tensor_tensor(stt[:, 5:6], stt[:, 3:4], LN_EPS, stt[:, 4:5], ALU.add, ALU.subtract), waits=[tb, tc])
            ts2 = k.step("scalar", lambda e: e.activation(stt[:, 6:7], stt[:, 5:6], AF.Sqrt), waits=[tv3])
            tv4 = k.step("vector", lambda e: e.reciprocal(stt[:, 7:8], stt[:, 6:7]), waits=[ts2])
            wv5 = [tv4, ta] + ([lastTr[bi]] if lastTr[bi] else [])
            tv5 = k.step("vector", lambda e, bi=bi: e.scalar_tensor_tensor(
                xn[:, bi * D:(bi + 1) * D], xbuf[:, bi * D:(bi + 1) * D], stt[:, 2:3], stt[:, 7:8].broadcast_to([128, D]),
                ALU.subtract, ALU.mult), waits=wv5)
            lastV[bi] = tv5
            sl = i % 2
            for g in range(2):
                w2 = [tv5]
                if lastT[g] is not None:
                    w2.append(lastT[g])
                tt = None
                for c8 in range(8):
                    cc = bi * DT + g * 8 + c8
                    tt = k.step("tensor", lambda e, cc=cc, g=g, c8=c8: e.transpose(
                        p.ps_t[:, g * 1024 + c8 * 128: g * 1024 + (c8 + 1) * 128],
                        xn[:, cc * 128:(cc + 1) * 128], p.ident[:]),
                        waits=w2 if c8 == 0 else [])
                lastTr[bi] = tt
                w3 = [tt]
                if lastST[sl] is not None:
                    w3.append(lastST[sl])
                lastT[g] = k.step("vector", lambda e, g=g, sl=sl: e.tensor_copy(
                    xnT[:, sl * D + g * 1024: sl * D + (g + 1) * 1024],
                    p.ps_t[:, g * 1024:(g + 1) * 1024]), waits=w3)
            lastST[sl] = k.step("sync", lambda e, i=i, sl=sl: e.dma_start(
                out=xn0[:, i * 128:(i + 1) * 128].rearrange("(c p) t -> p c t", p=128),
                in_=xnT[:, sl * D:(sl + 1) * D].rearrange("p (c t) -> p c t", c=16)), waits=[lastT[0], lastT[1]], dma=True)

def build_program(phase_names, io_spec, nphase_barriers=True):
    nc = bass.Bass("TRN2", target_bir_lowering=False)
    p = P()
    p.nc = nc
    p.io = {}
    for name, (shape, dt, kind) in io_spec.items():
        p.io[name] = nc.dram_tensor(name, list(shape), dt, kind=kind).ap()
    p.kb = KB(nc)
    import contextlib
    with contextlib.ExitStack() as stack:
        p.ps_f = stack.enter_context(nc.psum_tensor("ps_f", [128, 2560], F32))
        p.ps_x = stack.enter_context(nc.psum_tensor("ps_x", [128, 512], F32))
        p.ps_t = stack.enter_context(nc.psum_tensor("ps_t", [128, 2048], BF16))
        p.persist = stack
        sems = {}
        semctx = []
        for e in ENGINES:
            h = stack.enter_context(nc.semaphore("sem_" + e))
            sems[e] = h
        alloc_consts(p, stack)
        p.last_barrier = p.const_tok
        for ph in phase_names:
            PHASES[ph](p)
            bar = p.kb.barrier()
            p.last_barrier = bar
        with nc.Block() as block:
            p.kb.finalize(block, sems)
    return nc

PHASES = {"A": phase_A}


def const_io_spec():
    return {
        "c_ident": ((128, 128), F32, "ExternalInput"),
        "c_gt": ((128, C), F32, "ExternalInput"),
        "c_tri": ((128, C), F32, "ExternalInput"),
        "c_pramp": ((128, 2048), F32, "ExternalInput"),
        "c_prampend": ((128, C), F32, "ExternalInput"),
        "c_aper": ((128, 2048), F32, "ExternalInput"),
        "c_ones": ((128, 2), F32, "ExternalInput"),
        "c_ones_row": ((1, 128), F32, "ExternalInput"),
        "c_dmask": ((128, 4 * 512), F32, "ExternalInput"),
    }

def const_in_map(hp):
    return {k: hp[k[2:]] if False else hp["c_" + k[2:]] if False else hp[k.replace("c_", "c_", 1)] for k in []} or {
        "c_ident": hp["c_ident"], "c_gt": hp["c_gt"], "c_tri": hp["c_tri"],
        "c_pramp": hp["c_pramp"], "c_prampend": hp["c_prampend"],
        "c_aper": hp["c_aper"], "c_ones": hp["c_ones"], "c_ones_row": hp["c_ones_row"],
        "c_dmask": hp["c_dmask"],
    }

def phase_B(p):
    nc, k = p.nc, p.kb
    import contextlib
    with contextlib.ExitStack() as st:
        w_sb = st.enter_context(nc.sbuf_tensor("b_w", [128, DT * D], BF16))
        xc = st.enter_context(nc.sbuf_tensor("b_xc", [128, 2 * DT * 512], BF16))
        ob = st.enter_context(nc.sbuf_tensor("b_ob", [128, 2 * DT * 512], BF16))
        vb = st.enter_context(nc.sbuf_tensor("b_vb", [128, 8 * 512], BF16))
        bq = st.enter_context(nc.sbuf_tensor("b_bq", [128, 32], F32))
        bv = st.enter_context(nc.sbuf_tensor("b_bv", [1, D], BF16))
        xn0, v0 = p.io["xn0_cm"], p.io["v0_htm"]
        v0v = v0.rearrange("(h n) d -> h n d", h=H)
        bar = p.last_barrier
        k.step("sync", lambda e: e.dma_start(
            out=bq[:].rearrange("p (w j) -> p w j", w=2), in_=p.io["b_qkv"].transpose([1, 0, 2])), waits=bar, dma=True)
        tbv = k.step("gpsimd", lambda e: e.dma_start(out=bv[:], in_=p.io["b_v_row"]), waits=bar, dma=True)
        tbq = k.token("sync")

        gb = [0]
        bank_evac = {}
        lastW = None
        lastXc = [None, None]
        lastOb = [None, None]
        lastVb = [None] * 8

        def psum_bank():
            b = gb[0] % 5
            gb[0] += 1
            w = [bank_evac[b]] if b in bank_evac else []
            return b, w

        for wi, wname in enumerate(["w_q", "w_k", "w_v"]):
            w_hbm = p.io[wname].rearrange("(d p) m -> d p m", p=128)
            wld = []
            for d in range(DT):
                wv_ = [lastW] if (d == 0 and lastW is not None) else (bar if wi == 0 and d == 0 else [])
                wld.append(k.step("gpsimd", lambda e, d=d, w_hbm=w_hbm: e.dma_start(
                    out=w_sb[:, d * D:(d + 1) * D], in_=w_hbm[d]), waits=wv_, dma=True))
            wtok = wld[-1]
            for i in range(8):
                sl = i % 2
                wxc = [wtok] if i < 2 else [wtok, lastXc[sl]]
                txc = k.step("sync", lambda e, i=i, sl=sl: e.dma_start(
                    out=xc[:].rearrange("p (s d t) -> p s d t", s=2, d=DT)[:, sl],
                    in_=xn0[:, i * 512:(i + 1) * 512].rearrange("(d p) t -> p d t", p=128)),
                    waits=wxc, dma=True)
                if wi < 2:
                    evs = []
                    for j in range(DT):
                        b, wb = psum_bank()
                        mm = None
                        for d in range(DT):
                            mm = k.step("tensor", lambda e, d=d, j=j, sl=sl, b=b: e.matmul(
                                p.ps_f[:, b * 512:(b + 1) * 512],
                                w_sb[:, d * D + j * 128: d * D + (j + 1) * 128],
                                xc[:, (sl * DT + d) * 512:(sl * DT + d + 1) * 512],
                                start=(d == 0), stop=(d == DT - 1)),
                                waits=([txc] + wb) if d == 0 else [])
                        wev = [mm, tbq] + ([lastOb[sl]] if (j == 0 and lastOb[sl] is not None) else [])
                        ev = k.step("vector", lambda e, j=j, sl=sl, b=b, wi=wi: e.tensor_scalar(
                            ob[:, (sl * DT + j) * 512:(sl * DT + j + 1) * 512],
                            p.ps_f[:, b * 512:(b + 1) * 512],
                            bq[:, wi * 16 + j: wi * 16 + j + 1],
                            SCALE0 if wi == 0 else None,
                            ALU.add, ALU.mult if wi == 0 else ALU.bypass), waits=wev)
                        bank_evac[b] = ev
                        evs.append(ev)
                    out_cm = p.io["q0_cm" if wi == 0 else "k0_cm"]
                    lastOb[sl] = k.step("sync", lambda e, i=i, sl=sl, o=out_cm: e.dma_start(
                        out=o[:, i * 512:(i + 1) * 512].rearrange("(j p) t -> p j t", p=128),
                        in_=ob[:].rearrange("p (s j t) -> p s j t", s=2, j=DT)[:, sl]),
                        waits=evs, dma=True)
                    lastXc[sl] = k.token("tensor")
                else:
                    for tt in range(4):
                        for dc in range(4):
                            b, wb = psum_bank()
                            mm = None
                            for d in range(DT):
                                mm = k.step("tensor", lambda e, d=d, tt=tt, dc=dc, sl=sl, b=b: e.matmul(
                                    p.ps_f[:, b * 512:(b + 1) * 512],
                                    xc[:, (sl * DT + d) * 512 + tt * 128: (sl * DT + d) * 512 + (tt + 1) * 128],
                                    w_sb[:, d * D + dc * 512:d * D + (dc + 1) * 512],
                                    start=(d == 0), stop=False),
                                    waits=([txc, tbv] + wb) if d == 0 else [])
                            mm = k.step("tensor", lambda e, dc=dc, b=b: e.matmul(
                                p.ps_f[:, b * 512:(b + 1) * 512],
                                p.ones_row[:],
                                bv[:, dc * 512:(dc + 1) * 512],
                                start=False, stop=True))
                            vsl = (tt * 4 + dc) % 8
                            wev = [mm] + ([lastVb[vsl]] if lastVb[vsl] is not None else [])
                            ev = k.step("vector", lambda e, vsl=vsl, b=b: e.tensor_copy(
                                vb[:, vsl * 512:(vsl + 1) * 512], p.ps_f[:, b * 512:(b + 1) * 512]),
                                waits=wev)
                            bank_evac[b] = ev
                            lastVb[vsl] = k.step("sync", lambda e, i=i, tt=tt, dc=dc, vsl=vsl: e.dma_start(
                                out=v0v[4 * dc:4 * (dc + 1), i * 512 + tt * 128: i * 512 + (tt + 1) * 128, :].transpose([1, 0, 2]),
                                in_=vb[:, vsl * 512:(vsl + 1) * 512].rearrange("p (h d) -> p h d", h=4)),
                                waits=[ev], dma=True)
                    lastXc[sl] = k.token("tensor")
            lastW = k.token("tensor")

PHASES["B"] = phase_B

def phase_C(p):
    nc, k = p.nc, p.kb
    import contextlib
    with contextlib.ExitStack() as st:
        qh = st.enter_context(nc.sbuf_tensor("c_qh", [128, 2 * 2048], BF16))
        kh = st.enter_context(nc.sbuf_tensor("c_kh", [128, 2 * 2048], BF16))
        vh = st.enter_context(nc.sbuf_tensor("c_vh", [128, 2 * 16 * 132], BF16))
        pexp = st.enter_context(nc.sbuf_tensor("c_pexp", [128, 2 * 512], BF16))
        osb = st.enter_context(nc.sbuf_tensor("c_osb", [128, 2 * 2048], BF16))
        ocm = st.enter_context(nc.sbuf_tensor("c_ocm", [128, 2 * 2048], BF16))
        rc = st.enter_context(nc.sbuf_tensor("c_rc", [128, 16], F32))
        dmask = st.enter_context(nc.sbuf_tensor("c_dm", [128, 4 * 512], BF16))
        bar = p.last_barrier
        k.step("gpsimd", lambda e: e.dma_start(out=dmask[:], in_=p.io["c_dmask"]), waits=bar, dma=True)
        q0, k0, v0, o0 = p.io["q0_cm"], p.io["k0_cm"], p.io["v0_htm"], p.io["o0_cm"]

        ldtok = {}
        consumed = {}
        lastO = [None, None]
        lastOc = [None, None]
        lastPex = [None, None]
        lgEv = [None, None]
        oEv = [None] * 4

        def emit_loads(bh):
            b_, h_ = bh // H, bh % H
            sl = bh % 2
            w = list(bar) if bh < 2 else [consumed[bh - 2]]
            t1 = k.step("sync", lambda e, b_=b_, h_=h_, sl=sl: e.dma_start(
                out=qh[:, sl * 2048:(sl + 1) * 2048],
                in_=q0[h_ * 128:(h_ + 1) * 128, b_ * 2048:(b_ + 1) * 2048]), waits=w, dma=True)
            t2 = k.step("sync", lambda e, b_=b_, h_=h_, sl=sl: e.dma_start(
                out=kh[:, sl * 2048:(sl + 1) * 2048],
                in_=k0[h_ * 128:(h_ + 1) * 128, b_ * 2048:(b_ + 1) * 2048]), waits=[], dma=True)
            vsrc = v0[(h_ * NT + b_ * 2048) : (h_ * NT + (b_ + 1) * 2048), :]
            t3 = k.step("sync", lambda e, vsrc=vsrc, sl=sl: e.dma_start(
                out=vh[:].rearrange("p (s kt c) -> p s kt c", s=2, kt=16)[:, sl, 0:8, 0:128],
                in_=vsrc[0:1024].rearrange("(kt p) d -> p kt d", p=128)), waits=[], dma=True)
            t4 = k.step("sync", lambda e, vsrc=vsrc, sl=sl: e.dma_start(
                out=vh[:].rearrange("p (s kt c) -> p s kt c", s=2, kt=16)[:, sl, 8:16, 0:128],
                in_=vsrc[1024:2048].rearrange("(kt p) d -> p kt d", p=128)), waits=[], dma=True)
            t5 = k.step("vector", lambda e, sl=sl: e.memset(
                vh[:].rearrange("p (s kt c) -> p s kt c", s=2, kt=16)[:, sl, :, 128:129], 1.0),
                waits=w, ninc=1)
            ldtok[bh] = (t1, t2, t3, t4, t5)

        emit_loads(0)
        emit_loads(1)
        for bh in range(B * H):
            b_, h_ = bh // H, bh % H
            sl = bh % 2
            qv = lambda qs, sl=sl: qh[:, sl * 2048 + qs * 512: sl * 2048 + (qs + 1) * 512]
            kv = lambda kt, sl=sl: kh[:, sl * 2048 + kt * 128: sl * 2048 + (kt + 1) * 128]
            vv = lambda kt, sl=sl: vh[:, sl * 16 * 132 + kt * 132: sl * 16 * 132 + kt * 132 + 129]
            ld = ldtok.pop(bh)
            psO = [p.ps_f[:, 2 * 512:2 * 512 + 129], p.ps_f[:, 3 * 512:3 * 512 + 129],
                   p.ps_f[:, 4 * 512:4 * 512 + 129], p.ps_x[:, 0:129]]
            for qs in range(4):
                nk = 4 * qs + 4
                exps = []
                for kt in range(nk):
                    lb = kt % 2
                    w = [ld[0], ld[1]] + ([lgEv[lb]] if lgEv[lb] is not None else [])
                    tl = k.step("tensor", lambda e, kt=kt, qs=qs, lb=lb, qv=qv, kv=kv: e.matmul(
                        p.ps_f[:, lb * 512:(lb + 1) * 512], kv(kt), qv(qs),
                        start=True, stop=True), waits=w)
                    pslot = kt % 2
                    wx = [tl] + ([lastPex[pslot]] if lastPex[pslot] is not None else [])
                    te = k.step("scalar", lambda e, lb=lb, pslot=pslot: e.activation(
                        pexp[:, pslot * 512:(pslot + 1) * 512],
                        p.ps_f[:, lb * 512:(lb + 1) * 512], AF.Exp), waits=wx)
                    if kt >= 4 * qs:
                        c = kt - 4 * qs
                        te = k.step("vector", lambda e, c=c, pslot=pslot: e.tensor_tensor(
                            pexp[:, pslot * 512:(pslot + 1) * 512],
                            pexp[:, pslot * 512:(pslot + 1) * 512],
                            dmask[:, c * 512:(c + 1) * 512], ALU.mult), waits=[te])
                    lgEv[lb] = te
                    tpv = None
                    for c in range(4):
                        w2 = [te, ld[2], ld[3], ld[4]]
                        if kt == 0 and oEv[c] is not None:
                            w2.append(oEv[c])
                        tpv = k.step("tensor", lambda e, c=c, kt=kt, pslot=pslot, psO=psO, vv=vv, nk=nk: e.matmul(
                            psO[c][:, 0:129], pexp[:, pslot * 512 + c * 128: pslot * 512 + (c + 1) * 128],
                            vv(kt), start=(kt == 0), stop=(kt == nk - 1)), waits=w2)
                    lastPex[pslot] = tpv
                for c in range(4):
                    qt = qs * 4 + c
                    w3 = [k.token("tensor")] + ([lastO[sl]] if (qt == 0 and lastO[sl] is not None) else [])
                    tr = k.step("vector", lambda e, c=c, psO=psO, qt=qt: e.reciprocal(
                        rc[:, qt:qt + 1], psO[c][:, 128:129]), waits=w3)
                    tev = k.step("vector", lambda e, c=c, sl=sl, qt=qt, psO=psO: e.tensor_scalar(
                        osb[:, sl * 2048 + qt * 128: sl * 2048 + (qt + 1) * 128],
                        psO[c][:, 0:128], rc[:, qt:qt + 1], None, ALU.mult), waits=[tr])
                    oEv[c] = tev
            tgrp = []
            for g in range(2):
                w4 = [oEv[0], oEv[1], oEv[2], oEv[3]]
                if len(tgrp) == 0 and lastOc[sl] is not None:
                    pass
                tt = None
                for c8 in range(8):
                    qt = g * 8 + c8
                    tt = k.step("tensor", lambda e, qt=qt, g=g, c8=c8, sl=sl: e.transpose(
                        p.ps_t[:, g * 1024 + c8 * 128: g * 1024 + (c8 + 1) * 128],
                        osb[:, sl * 2048 + qt * 128: sl * 2048 + (qt + 1) * 128], p.ident[:]),
                        waits=(w4 + tgrp) if c8 == 0 else [])
                w5 = [tt] + ([lastOc[sl]] if lastOc[sl] is not None else [])
                ev = k.step("vector", lambda e, g=g, sl=sl: e.tensor_copy(
                    ocm[:, sl * 2048 + g * 1024: sl * 2048 + (g + 1) * 1024],
                    p.ps_t[:, g * 1024:(g + 1) * 1024]), waits=w5)
                tgrp.append(ev)
            lastO[sl] = k.token("tensor")
            lastOc[sl] = k.step("sync", lambda e, b_=b_, h_=h_, sl=sl: e.dma_start(
                out=o0[h_ * 128:(h_ + 1) * 128, b_ * 2048:(b_ + 1) * 2048],
                in_=ocm[:, sl * 2048:(sl + 1) * 2048]), waits=tgrp, dma=True)
            consumed[bh] = k.token("tensor")
            if bh + 2 < B * H:
                emit_loads(bh + 2)

PHASES["C"] = phase_C

def phase_D(p):
    nc, k = p.nc, p.kb
    import contextlib
    with contextlib.ExitStack() as st:
        wo = st.enter_context(nc.sbuf_tensor("d_wo", [128, DT * D], BF16))
        oc = st.enter_context(nc.sbuf_tensor("d_oc", [128, 2 * DT * 512], BF16))
        bo = st.enter_context(nc.sbuf_tensor("d_bo", [1, D], BF16))
        ev = st.enter_context(nc.sbuf_tensor("d_ev", [128, 2 * D], BF16))
        bar = p.last_barrier
        o0, s0 = p.io["o0_cm"], p.io["s0_tm"]
        w_hbm = p.io["w_o"].rearrange("(d p) m -> d p m", p=128)
        wld = [k.step("gpsimd", lambda e, d=d: e.dma_start(
            out=wo[:, d * D:(d + 1) * D], in_=w_hbm[d]), waits=(bar if d == 0 else []), dma=True)
            for d in range(DT)]
        tbo = k.step("gpsimd", lambda e: e.dma_start(out=bo[:], in_=p.io["b_o_row"]), dma=True)
        wtok = k.token("gpsimd")
        gb = [0]; bank_evac = {}
        lastXc = [None, None]; lastEv = [None, None]
        for i in range(8):
            sl = i % 2
            wxc = [wtok] + ([lastXc[sl]] if lastXc[sl] else (bar if i < 2 else []))
            txc = k.step("sync", lambda e, i=i, sl=sl: e.dma_start(
                out=oc[:].rearrange("p (s d t) -> p s d t", s=2, d=DT)[:, sl],
                in_=o0[:, i * 512:(i + 1) * 512].rearrange("(d p) t -> p d t", p=128)),
                waits=wxc, dma=True)
            for tt in range(4):
                evs = []
                for dc in range(4):
                    b = gb[0] % 5; gb[0] += 1
                    wb = [bank_evac[b]] if b in bank_evac else []
                    mm = None
                    for d in range(DT):
                        mm = k.step("tensor", lambda e, d=d, tt=tt, dc=dc, sl=sl, b=b: e.matmul(
                            p.ps_f[:, b * 512:(b + 1) * 512],
                            oc[:, (sl * DT + d) * 512 + tt * 128:(sl * DT + d) * 512 + (tt + 1) * 128],
                            wo[:, d * D + dc * 512:d * D + (dc + 1) * 512],
                            start=(d == 0), stop=False), waits=([txc] + wb) if d == 0 else [])
                    mm = k.step("tensor", lambda e, dc=dc, b=b: e.matmul(
                        p.ps_f[:, b * 512:(b + 1) * 512], p.ones_row[:],
                        bo[:, dc * 512:(dc + 1) * 512], start=False, stop=True))
                    wev = [mm] + ([lastEv[sl]] if (dc == 0 and lastEv[sl]) else [])
                    evt = k.step("vector", lambda e, dc=dc, sl=sl, b=b: e.tensor_copy(
                        ev[:, sl * D + dc * 512: sl * D + (dc + 1) * 512],
                        p.ps_f[:, b * 512:(b + 1) * 512]), waits=wev)
                    bank_evac[b] = evt
                    evs.append(evt)
                lastEv[sl] = k.step("sync", lambda e, i=i, tt=tt, sl=sl: e.dma_start(
                    out=s0[i * 512 + tt * 128: i * 512 + (tt + 1) * 128, :],
                    in_=ev[:, sl * D:(sl + 1) * D]), waits=evs, dma=True)
            lastXc[sl] = k.token("tensor")

PHASES["D"] = phase_D

def phase_E(p):
    nc, k = p.nc, p.kb
    import contextlib
    with contextlib.ExitStack() as st:
        wg = st.enter_context(nc.sbuf_tensor("e_wg", [128, DT * 1024], BF16))
        wu = st.enter_context(nc.sbuf_tensor("e_wu", [128, DT * 1024], BF16))
        xc = st.enter_context(nc.sbuf_tensor("e_xc", [128, 2 * DT * 512], BF16))
        hsb = st.enter_context(nc.sbuf_tensor("e_h", [128, 8 * 512], F32))
        sq = st.enter_context(nc.sbuf_tensor("e_sq", [128, 8 * 512], F32))
        sg = st.enter_context(nc.sbuf_tensor("e_sg", [128, 2 * 512], F32))
        cmp_ = st.enter_context(nc.sbuf_tensor("e_cmp", [128, 3 * 512], F32))
        hsp = st.enter_context(nc.sbuf_tensor("e_hsp", [128, 8 * 512], BF16))
        bar = p.last_barrier
        xn0 = p.io["xn0_cm"]
        hsp_h = p.io["hsp"]
        wgv = p.io["w_gate"].rearrange("(d p) m -> d p m", p=128)
        wuv = p.io["w_up"].rearrange("(d p) m -> d p m", p=128)
        gb = [0]; bank_evac = {}
        lastW = None; lastXc = [None, None]
        lastH = None; lastSg = [None, None]
        prevSec = None; prevStores = []
        for qbh in range(8):
            wld = []
            for d in range(DT):
                wv_ = ([lastW] if lastW else bar) if d == 0 else []
                wld.append(k.step("gpsimd", lambda e, d=d, qbh=qbh: e.dma_start(
                    out=wg[:, d * 1024:(d + 1) * 1024], in_=wgv[d, :, qbh * 1024:(qbh + 1) * 1024]),
                    waits=wv_, dma=True))
                wld.append(k.step("gpsimd", lambda e, d=d, qbh=qbh: e.dma_start(
                    out=wu[:, d * 1024:(d + 1) * 1024], in_=wuv[d, :, qbh * 1024:(qbh + 1) * 1024]),
                    waits=[], dma=True))
            wtok = wld[-1]
            for i in range(8):
                sl = i % 2
                wxc = [wtok] + ([lastXc[sl]] if lastXc[sl] else [])
                txc = k.step("sync", lambda e, i=i, sl=sl: e.dma_start(
                    out=xc[:].rearrange("p (s d t) -> p s d t", s=2, d=DT)[:, sl],
                    in_=xn0[:, i * 512:(i + 1) * 512].rearrange("(d p) t -> p d t", p=128)),
                    waits=wxc, dma=True)
                hts = []
                for jt in range(8):
                    bgate = gb[0] % 5; gb[0] += 1
                    bup = gb[0] % 5; gb[0] += 1
                    wbg = [bank_evac[bgate]] if bgate in bank_evac else []
                    wbu = [bank_evac[bup]] if bup in bank_evac else []
                    mmg = None
                    for d in range(DT):
                        mmg = k.step("tensor", lambda e, d=d, jt=jt, sl=sl, b=bgate: e.matmul(
                            p.ps_f[:, b * 512:(b + 1) * 512],
                            wg[:, d * 1024 + jt * 128:d * 1024 + (jt + 1) * 128],
                            xc[:, (sl * DT + d) * 512:(sl * DT + d + 1) * 512],
                            start=(d == 0), stop=(d == DT - 1)), waits=([txc] + wbg) if d == 0 else [])
                    mmu = None
                    for d in range(DT):
                        mmu = k.step("tensor", lambda e, d=d, jt=jt, sl=sl, b=bup: e.matmul(
                            p.ps_f[:, b * 512:(b + 1) * 512],
                            wu[:, d * 1024 + jt * 128:d * 1024 + (jt + 1) * 128],
                            xc[:, (sl * DT + d) * 512:(sl * DT + d + 1) * 512],
                            start=(d == 0), stop=(d == DT - 1)), waits=wbu if d == 0 else [])
                    ssl = jt % 2
                    wsg = [mmg] + ([lastSg[ssl]] if lastSg[ssl] else [])
                    tsg = k.step("scalar", lambda e, ssl=ssl, b=bgate: e.activation(
                        sg[:, ssl * 512:(ssl + 1) * 512], p.ps_f[:, b * 512:(b + 1) * 512],
                        AF.Silu), waits=wsg)
                    wh = [tsg, mmu] + ([lastH] if (jt == 0 and lastH) else [])
                    th = k.step("vector", lambda e, jt=jt, ssl=ssl, b=bup: e.tensor_tensor(
                        hsb[:, jt * 512:(jt + 1) * 512], sg[:, ssl * 512:(ssl + 1) * 512],
                        p.ps_f[:, b * 512:(b + 1) * 512], ALU.mult), waits=wh)
                    lastSg[ssl] = th
                    bank_evac[bgate] = th
                    bank_evac[bup] = th
                    wq_ = [th] + ([prevSec] if (jt == 0 and prevSec) else [])
                    tq = k.step("scalar", lambda e, jt=jt: e.activation(
                        sq[:, jt * 512:(jt + 1) * 512], hsb[:, jt * 512:(jt + 1) * 512],
                        AF.Abs), waits=wq_)
                    hts.append(tq)
                spt = []
                for uw in range(2):
                    S = lambda j_, uw=uw: sq[:, (2 * j_ + uw) * 512:(2 * j_ + uw + 1) * 512]
                    hi1 = k.step("vector", lambda e, S=S: e.tensor_max(cmp_[:, 0:512], S(0), S(1)), waits=hts)
                    lo1 = k.step("vector", lambda e, S=S: e.tensor_tensor(cmp_[:, 512:1024], S(0), S(1), ALU.min), waits=[])
                    hi2 = k.step("vector", lambda e, S=S: e.tensor_max(cmp_[:, 1024:1536], S(2), S(3)), waits=[])
                    m1 = k.step("vector", lambda e, S=S: e.tensor_tensor(cmp_[:, 0:512], cmp_[:, 0:512], cmp_[:, 1024:1536], ALU.min), waits=[hi1, hi2])
                    lo2 = k.step("vector", lambda e, S=S: e.tensor_tensor(cmp_[:, 1024:1536], S(2), S(3), ALU.min), waits=[hi2])
                    m2 = k.step("vector", lambda e: e.tensor_max(cmp_[:, 512:1024], cmp_[:, 512:1024], cmp_[:, 1024:1536]), waits=[lo1, lo2])
                    sec = k.step("vector", lambda e: e.tensor_max(cmp_[:, 0:512], cmp_[:, 0:512], cmp_[:, 512:1024]), waits=[m1, m2])
                    for j_ in range(4):
                        jt = 2 * j_ + uw
                        tg = k.step("vector", lambda e, jt=jt, S=S, j_=j_: e.tensor_tensor(
                            sq[:, jt * 512:(jt + 1) * 512], S(j_), cmp_[:, 0:512], ALU.is_ge), waits=[sec])
                        tm_ = k.step("vector", lambda e, jt=jt: e.tensor_tensor(
                            hsp[:, jt * 512:(jt + 1) * 512], hsb[:, jt * 512:(jt + 1) * 512],
                            sq[:, jt * 512:(jt + 1) * 512], ALU.mult), waits=[tg] + prevStores)
                        spt.append(tm_)
                    prevSec = sec
                sts = []
                for jt in range(8):
                    dhg = qbh * 8 + jt
                    sts.append(k.step("sync", lambda e, i=i, dhg=dhg, jt=jt: e.dma_start(
                        out=hsp_h[(i * 64 + dhg) * 128:(i * 64 + dhg + 1) * 128, :],
                        in_=hsp[:, jt * 512:(jt + 1) * 512]), waits=spt, dma=True))
                lastXc[sl] = k.token("tensor")
                lastH = k.token("vector")
                prevStores = [sts[-1]]
            lastW = k.token("tensor")

PHASES["E"] = phase_E

def phase_F1(p):
    nc, k = p.nc, p.kb
    import contextlib
    with contextlib.ExitStack() as st:
        wd = st.enter_context(nc.sbuf_tensor("f_wd", [128, 64 * 512], BF16))
        hs = st.enter_context(nc.sbuf_tensor("f_hs", [128, 64 * 512], BF16))
        gv = st.enter_context(nc.sbuf_tensor("f_gv", [128, 4 * 512], BF16))
        bar = p.last_barrier
        hsp_h, g0 = p.io["hsp"], p.io["g0_tm"]
        wdv = p.io["w_down"].rearrange("(d p) m -> d p m", p=128)
        gb = [0]; bank_evac = {}
        lastHs = None; lastGv = [None] * 4
        for quarter in range(4):
            wld = []
            for d in range(64):
                wv_ = bar if (quarter == 0 and d == 0) else ([lastHs] if (d == 0 and lastHs) else [])
                wld.append(k.step("gpsimd", lambda e, d=d, quarter=quarter: e.dma_start(
                    out=wd[:, d * 512:(d + 1) * 512], in_=wdv[d, :, quarter * 512:(quarter + 1) * 512]),
                    waits=wv_, dma=True))
            wtok = wld[-1]
            for i in range(8):
                hld = []
                for d8 in range(8):
                    wh = [wtok] + ([lastHs] if (d8 == 0 and lastHs) else [])
                    hld.append(k.step("sync", lambda e, i=i, d8=d8: e.dma_start(
                        out=hs[:].rearrange("p (d t) -> p d t", d=64)[:, d8 * 8:(d8 + 1) * 8],
                        in_=hsp_h[(i * 64 + d8 * 8) * 128:(i * 64 + (d8 + 1) * 8) * 128, :]
                        .rearrange("(d p) t -> p d t", p=128)), waits=wh, dma=True))
                htok = hld[-1]
                for tt in range(4):
                    b = gb[0] % 5; gb[0] += 1
                    wb = [bank_evac[b]] if b in bank_evac else []
                    mm = None
                    for d in range(64):
                        mm = k.step("tensor", lambda e, d=d, tt=tt, b=b: e.matmul(
                            p.ps_f[:, b * 512:(b + 1) * 512],
                            hs[:, d * 512 + tt * 128:d * 512 + (tt + 1) * 128],
                            wd[:, d * 512:(d + 1) * 512],
                            start=(d == 0), stop=(d == 63)), waits=([htok] + wb) if d == 0 else [])
                    gsl = tt
                    wev = [mm] + ([lastGv[gsl]] if lastGv[gsl] else [])
                    evt = k.step("scalar", lambda e, gsl=gsl, b=b: e.activation(
                        gv[:, gsl * 512:(gsl + 1) * 512],
                        p.ps_f[:, b * 512:(b + 1) * 512], AF.Silu), waits=wev)
                    bank_evac[b] = evt
                    lastGv[gsl] = k.step("sync", lambda e, i=i, tt=tt, quarter=quarter, gsl=gsl: e.dma_start(
                        out=g0[i * 512 + tt * 128: i * 512 + (tt + 1) * 128,
                               quarter * 512:(quarter + 1) * 512],
                        in_=gv[:, gsl * 512:(gsl + 1) * 512]), waits=[evt], dma=True)
                lastHs = k.token("tensor")

PHASES["F1"] = phase_F1

def phase_F2(p):
    nc, k = p.nc, p.kb
    import contextlib
    with contextlib.ExitStack() as st:
        xbuf = st.enter_context(nc.sbuf_tensor("f2_x", [128, 2 * D], F32))
        sbuf = st.enter_context(nc.sbuf_tensor("f2_s", [128, 2 * D], BF16))
        gbuf = st.enter_context(nc.sbuf_tensor("f2_g", [128, 2 * D], BF16))
        x1b = st.enter_context(nc.sbuf_tensor("f2_x1", [128, 2 * D], F32))
        x2 = st.enter_context(nc.sbuf_tensor("f2_x2", [128, 2 * D], F32))
        xn = st.enter_context(nc.sbuf_tensor("f2_xn", [128, 2 * D], BF16))
        xnT = st.enter_context(nc.sbuf_tensor("f2_xnT", [128, 2 * D], BF16))
        stt = st.enter_context(nc.sbuf_tensor("f2_st", [128, 8], F32))
        x_hbm, s0, g0 = p.io["x"], p.io["s0_tm"], p.io["g0_tm"]
        x1_h, xn1 = p.io["x1_tm"], p.io["xn1_cm"]
        bar = p.last_barrier
        lastV = [None, None]; lastT = [None, None]; lastST = [None, None]
        lastX1 = [None, None]; lastTv2 = [None, None]; lastTr = [None, None]
        for i in range(NT // 128):
            bi = i % 2
            w = list(bar) if i < 2 else [lastV[bi]]
            tl = k.step("gpsimd", lambda e, i=i, bi=bi: e.dma_start(
                out=xbuf[:, bi * D:(bi + 1) * D], in_=x_hbm[i * 128:(i + 1) * 128, :]), waits=w, dma=True)
            tls = k.step("sync", lambda e, i=i, bi=bi: e.dma_start(
                out=sbuf[:, bi * D:(bi + 1) * D], in_=s0[i * 128:(i + 1) * 128, :]), waits=w, dma=True)
            tlg = k.step("sync", lambda e, i=i, bi=bi: e.dma_start(
                out=gbuf[:, bi * D:(bi + 1) * D], in_=g0[i * 128:(i + 1) * 128, :]), waits=[], dma=True)
            wx1 = [tl, tls, tlg] + ([lastX1[bi]] if lastX1[bi] else [])
            tv0 = k.step("vector", lambda e, bi=bi: e.tensor_tensor(
                x2[:, bi * D:(bi + 1) * D], sbuf[:, bi * D:(bi + 1) * D],
                gbuf[:, bi * D:(bi + 1) * D], ALU.mult), waits=wx1)
            tv1_ = k.step("vector", lambda e, bi=bi: e.tensor_tensor(
                x1b[:, bi * D:(bi + 1) * D], xbuf[:, bi * D:(bi + 1) * D],
                x2[:, bi * D:(bi + 1) * D], ALU.add), waits=[tv0])
            tstx = k.step("sync", lambda e, i=i, bi=bi: e.dma_start(
                out=x1_h[i * 128:(i + 1) * 128, :], in_=x1b[:, bi * D:(bi + 1) * D]), waits=[tv1_], dma=True)
            tv1 = k.step("vector", lambda e, bi=bi: e.reduce_sum(
                stt[:, 0:1], x1b[:, bi * D:(bi + 1) * D], axis=AX.X), waits=[tv1_])
            ws1 = [tv1_] + ([lastTv2[bi]] if lastTv2[bi] else [])
            ts1 = k.step("scalar", lambda e, bi=bi: e.activation(
                x2[:, bi * D:(bi + 1) * D], x1b[:, bi * D:(bi + 1) * D], AF.Square), waits=ws1)
            tv2 = k.step("vector", lambda e, bi=bi: e.reduce_sum(
                stt[:, 1:2], x2[:, bi * D:(bi + 1) * D], axis=AX.X), waits=[ts1])
            lastTv2[bi] = tv2
            ta = k.step("vector", lambda e: e.tensor_scalar(stt[:, 2:3], stt[:, 0:1], 1.0 / D, None, ALU.mult), waits=[tv1])
            tb = k.step("vector", lambda e: e.tensor_scalar(stt[:, 3:4], stt[:, 1:2], 1.0 / D, None, ALU.mult), waits=[tv2])
            tc = k.step("vector", lambda e: e.tensor_tensor(stt[:, 4:5], stt[:, 2:3], stt[:, 2:3], ALU.mult), waits=[ta])
            tv3 = k.step("vector", lambda e: e.scalar_tensor_tensor(stt[:, 5:6], stt[:, 3:4], LN_EPS, stt[:, 4:5], ALU.add, ALU.subtract), waits=[tb, tc])
            ts2 = k.step("scalar", lambda e: e.activation(stt[:, 6:7], stt[:, 5:6], AF.Sqrt), waits=[tv3])
            tv4 = k.step("vector", lambda e: e.reciprocal(stt[:, 7:8], stt[:, 6:7]), waits=[ts2])
            wv5 = [tv4, ta] + ([lastTr[bi]] if lastTr[bi] else [])
            tv5 = k.step("vector", lambda e, bi=bi: e.scalar_tensor_tensor(
                xn[:, bi * D:(bi + 1) * D], x1b[:, bi * D:(bi + 1) * D], stt[:, 2:3],
                stt[:, 7:8].broadcast_to([128, D]), ALU.subtract, ALU.mult), waits=wv5)
            lastV[bi] = tv5
            lastX1[bi] = tstx
            sl = i % 2
            tg_ = []
            for g in range(2):
                w2 = [tv5] + ([lastT[g]] if lastT[g] else [])
                tt = None
                for c8 in range(8):
                    cc = bi * DT + g * 8 + c8
                    tt = k.step("tensor", lambda e, cc=cc, g=g, c8=c8: e.transpose(
                        p.ps_t[:, g * 1024 + c8 * 128: g * 1024 + (c8 + 1) * 128],
                        xn[:, cc * 128:(cc + 1) * 128], p.ident[:]), waits=w2 if c8 == 0 else [])
                lastTr[bi] = tt
                w3 = [tt] + ([lastST[sl]] if lastST[sl] else [])
                lastT[g] = k.step("vector", lambda e, g=g, sl=sl: e.tensor_copy(
                    xnT[:, sl * D + g * 1024: sl * D + (g + 1) * 1024],
                    p.ps_t[:, g * 1024:(g + 1) * 1024]), waits=w3)
                tg_.append(lastT[g])
            lastST[sl] = k.step("sync", lambda e, i=i, sl=sl: e.dma_start(
                out=xn1[:, i * 128:(i + 1) * 128].rearrange("(c p) t -> p c t", p=128),
                in_=xnT[:, sl * D:(sl + 1) * D].rearrange("p (c t) -> p c t", c=16)), waits=tg_, dma=True)

PHASES["F2"] = phase_F2

def phase_G(p):
    nc, k = p.nc, p.kb
    import contextlib
    with contextlib.ExitStack() as st:
        w_sb = st.enter_context(nc.sbuf_tensor("g_w", [128, DT * D], BF16))
        xc = st.enter_context(nc.sbuf_tensor("g_xc", [128, 2 * DT * 512], BF16))
        ob = st.enter_context(nc.sbuf_tensor("g_ob", [128, 2 * DT * 512], BF16))
        cb = st.enter_context(nc.sbuf_tensor("g_cb", [128, 2 * 515], F32))
        yv = st.enter_context(nc.sbuf_tensor("g_yv", [128, 2 * 512], F32))
        tails = st.enter_context(nc.sbuf_tensor("g_tails", [128, DT * 3], F32))
        cv = st.enter_context(nc.sbuf_tensor("g_cv", [128, 3 * 64], F32))
        wsm = st.enter_context(nc.sbuf_tensor("g_wsm", [128, 3 * 16 * 16], BF16))
        f1 = st.enter_context(nc.sbuf_tensor("g_f1", [128, 2 * 512], BF16))
        gd = st.enter_context(nc.sbuf_tensor("g_gd", [128, 3 * 512], F32))
        bcm = st.enter_context(nc.sbuf_tensor("g_bcm", [16, 2 * 512], F32))
        xgb = st.enter_context(nc.sbuf_tensor("g_xgb", [128, 2 * 512], BF16))
        bar = p.last_barrier
        xn1 = p.io["xn1_cm"]
        tcv = k.step("sync", lambda e: e.dma_start(out=cv[:], in_=p.io["l1_conv"]), waits=bar, dma=True)
        wf1 = st.enter_context(nc.sbuf_tensor("g_wf1", [128, DT * 128], BF16))
        wg1 = st.enter_context(nc.sbuf_tensor("g_wg1", [128, DT * 128], BF16))
        wf2 = st.enter_context(nc.sbuf_tensor("g_wf2", [128, DT * 128], BF16))
        twb = k.step("gpsimd", lambda e: e.dma_start(
            out=wsm[:, 0:256].rearrange("p (d j) -> p d j", d=DT),
            in_=p.io["l1_Wb"].rearrange("(d p) j -> p d j", p=128)), waits=bar, dma=True)
        twf1 = k.step("gpsimd", lambda e: e.dma_start(
            out=wf1[:].rearrange("p (d f) -> p d f", d=DT),
            in_=p.io["l1_Wf1"].rearrange("(d p) f -> p d f", p=128)), waits=[], dma=True)
        twg1 = k.step("gpsimd", lambda e: e.dma_start(
            out=wg1[:].rearrange("p (d f) -> p d f", d=DT),
            in_=p.io["l1_Wg1"].rearrange("(d p) f -> p d f", p=128)), waits=[], dma=True)
        twf2 = k.step("gpsimd", lambda e: e.dma_start(
            out=wf2[:], in_=p.io["l1_Wf2"]), waits=[], dma=True)
        small_tok = [twb, twf1, twg1, twf2, tcv]

        gb = [0]; bank_evac = {}
        lastW = None; lastXc = [None, None]; lastOb = [None, None]
        lastCb = [None, None]; lastYv = [None, None]; lastF1 = [None, None]
        lastGd = [None, None, None]
        lastBcm = [None, None]; lastXg = [None, None]; lastF1tok = [None, None]
        names = [("l1_Wq", "q1_cm", 0), ("l1_Wk", "k1_cm", 1), ("l1_Wv", "v1_cm", 2), ("l1_Ww", "w1_cm", 2)]
        for wi, (wname, oname, kk) in enumerate(names):
            w_hbm = p.io[wname].rearrange("(d p) m -> d p m", p=128)
            wld = []
            for d in range(DT):
                wv_ = ([lastW] if lastW else bar) if d == 0 else []
                wld.append(k.step("gpsimd", lambda e, d=d, w_hbm=w_hbm: e.dma_start(
                    out=w_sb[:, d * D:(d + 1) * D], in_=w_hbm[d]), waits=wv_, dma=True))
            wtok = wld[-1]
            tzt = k.step("vector", lambda e: e.memset(tails[:], 0.0), waits=[lastW] if lastW else [])
            for i in range(8):
                sl = i % 2
                wxc = [wtok] + ([lastXc[sl]] if lastXc[sl] else [])
                txc = k.step("sync", lambda e, i=i, sl=sl: e.dma_start(
                    out=xc[:].rearrange("p (s d t) -> p s d t", s=2, d=DT)[:, sl],
                    in_=xn1[:, i * 512:(i + 1) * 512].rearrange("(d p) t -> p d t", p=128)),
                    waits=wxc, dma=True)
                if i % 4 == 0:
                    tzt = k.step("vector", lambda e: e.memset(tails[:], 0.0),
                                 waits=[lastCb[0], lastCb[1]] if lastCb[0] else [])
                evs = []
                for j in range(DT):
                    b = gb[0] % 5; gb[0] += 1
                    wb = [bank_evac[b]] if b in bank_evac else []
                    mm = None
                    for d in range(DT):
                        mm = k.step("tensor", lambda e, d=d, j=j, sl=sl, b=b: e.matmul(
                            p.ps_f[:, b * 512:(b + 1) * 512],
                            w_sb[:, d * D + j * 128: d * D + (j + 1) * 128],
                            xc[:, (sl * DT + d) * 512:(sl * DT + d + 1) * 512],
                            start=(d == 0), stop=(d == DT - 1)), waits=([txc] + wb) if d == 0 else [])
                    cs = j % 2
                    wcv = [mm, tcv] + ([lastCb[cs]] if lastCb[cs] else [])
                    t1 = k.step("vector", lambda e, cs=cs, b=b: e.tensor_copy(
                        cb[:, cs * 515 + 3: (cs + 1) * 515], p.ps_f[:, b * 512:(b + 1) * 512]), waits=wcv)
                    bank_evac[b] = t1
                    t2 = k.step("vector", lambda e, cs=cs, j=j: e.tensor_copy(
                        cb[:, cs * 515: cs * 515 + 3], tails[:, j * 3:(j + 1) * 3]), waits=[t1, tzt])
                    t3 = k.step("vector", lambda e, cs=cs, j=j: e.tensor_copy(
                        tails[:, j * 3:(j + 1) * 3], cb[:, cs * 515 + 512: cs * 515 + 515]), waits=[t2])
                    ys = j % 2
                    cvc = lambda tap, kk=kk, j=j: cv[:, kk * 64 + j * 4 + tap: kk * 64 + j * 4 + tap + 1]
                    wy = [t2] + ([lastYv[ys]] if lastYv[ys] else [])
                    ty = k.step("vector", lambda e, cs=cs, ys=ys, cvc=cvc: e.tensor_scalar(
                        yv[:, ys * 512:(ys + 1) * 512], cb[:, cs * 515 + 3:(cs + 1) * 515],
                        cvc(3), None, ALU.mult), waits=wy)
                    for tap in range(3):
                        ty = k.step("vector", lambda e, cs=cs, ys=ys, tap=tap, cvc=cvc: e.scalar_tensor_tensor(
                            yv[:, ys * 512:(ys + 1) * 512], cb[:, cs * 515 + tap: cs * 515 + tap + 512],
                            cvc(tap), yv[:, ys * 512:(ys + 1) * 512], ALU.mult, ALU.add), waits=[ty])
                    lastCb[cs] = ty
                    wsi = [ty] + ([lastOb[sl]] if (j == 0 and lastOb[sl]) else [])
                    tso = k.step("scalar", lambda e, sl=sl, j=j, ys=ys: e.activation(
                        ob[:, (sl * DT + j) * 512:(sl * DT + j + 1) * 512],
                        yv[:, ys * 512:(ys + 1) * 512], AF.Silu), waits=wsi)
                    lastYv[ys] = tso
                    if wi == 0:
                        tso = k.step("vector", lambda e, sl=sl, j=j: e.tensor_scalar(
                            ob[:, (sl * DT + j) * 512:(sl * DT + j + 1) * 512],
                            ob[:, (sl * DT + j) * 512:(sl * DT + j + 1) * 512],
                            SCALE1, None, ALU.mult), waits=[tso])
                    evs.append(tso)
                out_cm = p.io[oname]
                lastOb[sl] = k.step("sync", lambda e, i=i, sl=sl, o=out_cm: e.dma_start(
                    out=o[:, i * 512:(i + 1) * 512].rearrange("(j p) t -> p j t", p=128),
                    in_=ob[:].rearrange("p (s j t) -> p s j t", s=2, j=DT)[:, sl]),
                    waits=evs, dma=True)
                lastXc[sl] = k.token("tensor")
                if wi == 0:
                    bsl = gb[0] % 5; gb[0] += 1
                    wbk = [bank_evac[bsl]] if bsl in bank_evac else []
                    mmb = None
                    for d in range(DT):
                        mmb = k.step("tensor", lambda e, d=d, sl=sl, b=bsl: e.matmul(
                            p.ps_f[:, b * 512:b * 512 + 512][0:16, :],
                            wsm[:, d * 16:(d + 1) * 16],
                            xc[:, (sl * DT + d) * 512:(sl * DT + d + 1) * 512],
                            start=(d == 0), stop=(d == DT - 1)), waits=(wbk + [twb]) if d == 0 else [])
                    csl = i % 2
                    wse = [mmb] + ([lastBcm[csl]] if lastBcm[csl] else [])
                    tse = k.step("scalar", lambda e, b=bsl, csl=csl: e.activation(
                        bcm[:, csl * 512:(csl + 1) * 512], p.ps_f[0:16, b * 512:(b + 1) * 512],
                        AF.Sigmoid), waits=wse)
                    bank_evac[bsl] = tse
                    lastBcm[csl] = k.step("sync", lambda e, i=i, csl=csl: e.dma_start(
                        out=p.io["beta_cm"][:, i * 512:(i + 1) * 512], in_=bcm[:, csl * 512:(csl + 1) * 512]),
                        waits=[tse], dma=True)
                    bf_ = gb[0] % 5; gb[0] += 1
                    wbk = [bank_evac[bf_]] if bf_ in bank_evac else []
                    mmf = None
                    for d in range(DT):
                        mmf = k.step("tensor", lambda e, d=d, sl=sl, b=bf_: e.matmul(
                            p.ps_f[:, b * 512:(b + 1) * 512],
                            wf1[:, d * 128:(d + 1) * 128],
                            xc[:, (sl * DT + d) * 512:(sl * DT + d + 1) * 512],
                            start=(d == 0), stop=(d == DT - 1)), waits=(wbk + [twf1]) if d == 0 else [])
                    fsl = i % 2
                    wf_ = [mmf] + ([lastF1tok[fsl]] if lastF1tok[fsl] else [])
                    tfe = k.step("vector", lambda e, b=bf_, fsl=fsl: e.tensor_copy(
                        f1[:, fsl * 512:(fsl + 1) * 512], p.ps_f[:, b * 512:(b + 1) * 512]), waits=wf_)
                    bank_evac[bf_] = tfe
                    bg_ = gb[0] % 5; gb[0] += 1
                    wbk = [bank_evac[bg_]] if bg_ in bank_evac else []
                    mmg = None
                    for d in range(DT):
                        mmg = k.step("tensor", lambda e, d=d, sl=sl, b=bg_: e.matmul(
                            p.ps_f[:, b * 512:(b + 1) * 512],
                            wg1[:, d * 128:(d + 1) * 128],
                            xc[:, (sl * DT + d) * 512:(sl * DT + d + 1) * 512],
                            start=(d == 0), stop=(d == DT - 1)), waits=(wbk + [twg1]) if d == 0 else [])
                    csl = i % 2
                    wge = [mmg] + ([lastXg[csl]] if lastXg[csl] else [])
                    tge = k.step("vector", lambda e, b=bg_, csl=csl: e.tensor_copy(
                        xgb[:, csl * 512:(csl + 1) * 512], p.ps_f[:, b * 512:(b + 1) * 512]), waits=wge)
                    bank_evac[bg_] = tge
                    lastXg[csl] = k.step("sync", lambda e, i=i, csl=csl: e.dma_start(
                        out=p.io["xg1_cm"][:, i * 512:(i + 1) * 512],
                        in_=xgb[:, csl * 512:(csl + 1) * 512]), waits=[tge], dma=True)
                    for h in range(H):
                        bz = gb[0] % 5; gb[0] += 1
                        wbk = [bank_evac[bz]] if bz in bank_evac else []
                        mmz = k.step("tensor", lambda e, h=h, fsl=fsl, b=bz: e.matmul(
                            p.ps_f[:, b * 512:(b + 1) * 512],
                            wf2[:, h * 128:(h + 1) * 128],
                            f1[:, fsl * 512:(fsl + 1) * 512],
                            start=True, stop=True), waits=([tfe, twf2] + wbk))
                        gsl_ = h % 2
                        wz = [mmz] + ([lastGd[gsl_]] if lastGd[gsl_] else [])
                        tn = k.step("vector", lambda e, b=bz, gsl_=gsl_: e.tensor_single_scalar(
                            gd[:, (gsl_ + 1) * 512:(gsl_ + 2) * 512], p.ps_f[:, b * 512:(b + 1) * 512],
                            0.0, ALU.min), waits=wz)
                        ta_ = k.step("scalar", lambda e, b=bz: e.activation(
                            gd[:, 0:512], p.ps_f[:, b * 512:(b + 1) * 512], AF.Abs), waits=wz + [tn])
                        te_ = k.step("scalar", lambda e: e.activation(
                            gd[:, 0:512], gd[:, 0:512], AF.Exp, scale=-1.0), waits=[ta_])
                        tl_ = k.step("scalar", lambda e: e.activation(
                            gd[:, 0:512], gd[:, 0:512], AF.Ln, bias=p.cc[:, 0:1]), waits=[te_])
                        tc_ = k.step("vector", lambda e, gsl_=gsl_: e.scalar_tensor_tensor(
                            gd[:, (gsl_ + 1) * 512:(gsl_ + 2) * 512],
                            gd[:, (gsl_ + 1) * 512:(gsl_ + 2) * 512],
                            -ALPHA, gd[:, 0:512], ALU.add, ALU.subtract), waits=[tl_, tn])
                        bank_evac[bz] = tc_
                        lastGd[gsl_] = k.step("sync", lambda e, i=i, h=h, gsl_=gsl_: e.dma_start(
                            out=p.io["glogdev_cm"][h * 128:(h + 1) * 128, i * 512:(i + 1) * 512],
                            in_=gd[:, (gsl_ + 1) * 512:(gsl_ + 2) * 512]), waits=[tc_], dma=True)
                    lastF1tok[fsl] = k.token("tensor")
            lastW = k.token("tensor")

PHASES["G"] = phase_G

def phase_H(p):
    nc, k = p.nc, p.kb
    import contextlib
    with contextlib.ExitStack() as st:
        qb_ = st.enter_context(nc.sbuf_tensor("h_q", [128, 2 * T], BF16))
        kb_ = st.enter_context(nc.sbuf_tensor("h_k", [128, 2 * T], BF16))
        vb_ = st.enter_context(nc.sbuf_tensor("h_v", [128, 2 * T], BF16))
        wb_ = st.enter_context(nc.sbuf_tensor("h_w", [128, 2 * T], BF16))
        gdb = st.enter_context(nc.sbuf_tensor("h_gd", [128, 2 * T], F32))
        brow = st.enter_context(nc.sbuf_tensor("h_brow", [1, 2 * T], F32))
        xgs = st.enter_context(nc.sbuf_tensor("h_xg", [128, T], BF16))
        wg2 = st.enter_context(nc.sbuf_tensor("h_wg2", [128, D], BF16))
        bg2 = st.enter_context(nc.sbuf_tensor("h_bg2", [128, 16], F32))
        nwc = st.enter_context(nc.sbuf_tensor("h_nw", [128, 1], F32))
        lam = st.enter_context(nc.sbuf_tensor("h_lam", [128, T], F32))
        tmpf = st.enter_context(nc.sbuf_tensor("h_tmpf", [128, T], F32))
        E1 = st.enter_context(nc.sbuf_tensor("h_E1", [128, T], BF16))
        E1i = st.enter_context(nc.sbuf_tensor("h_E1i", [128, T], BF16))
        W2e = st.enter_context(nc.sbuf_tensor("h_W2e", [128, T], BF16))
        hw = st.enter_context(nc.sbuf_tensor("h_hw", [128, T], BF16))
        W1 = st.enter_context(nc.sbuf_tensor("h_W1", [128, T], BF16))
        Wend = st.enter_context(nc.sbuf_tensor("h_Wend", [128, T], BF16))
        q2 = st.enter_context(nc.sbuf_tensor("h_q2", [128, T], BF16))
        cols = st.enter_context(nc.sbuf_tensor("h_cols", [128, 8 * 16], F32))
        rowf = st.enter_context(nc.sbuf_tensor("h_rowf", [1, T], F32))
        rowb = st.enter_context(nc.sbuf_tensor("h_rowb", [1, T], BF16))
        sqw = st.enter_context(nc.sbuf_tensor("h_sqw", [128, T], BF16))
        SkT = st.enter_context(nc.sbuf_tensor("h_SkT", [128, 128], F32))
        Sv = st.enter_context(nc.sbuf_tensor("h_Sv", [128, 128], F32))
        SkTb = st.enter_context(nc.sbuf_tensor("h_SkTb", [128, 128], BF16))
        Svb = st.enter_context(nc.sbuf_tensor("h_Svb", [128, 128], BF16))
        PG = st.enter_context(nc.sbuf_tensor("h_PG", [128, 128], BF16))
        Zg = st.enter_context(nc.sbuf_tensor("h_Zg", [128, 128], BF16))
        ssb = st.enter_context(nc.sbuf_tensor("h_ssb", [128, 128], F32))
        abf = st.enter_context(nc.sbuf_tensor("h_abf", [128, 128], BF16))
        atl = st.enter_context(nc.sbuf_tensor("h_atl", [128, 128], BF16))
        atp = st.enter_context(nc.sbuf_tensor("h_atp", [128, 128], BF16))
        trsp = st.enter_context(nc.sbuf_tensor("h_trsp", [128, 5 * 128], BF16))
        o2s = st.enter_context(nc.sbuf_tensor("h_o2", [128, 128], BF16))
        gts = st.enter_context(nc.sbuf_tensor("h_gts", [128, 128], BF16))
        tb_s = st.enter_context(nc.sbuf_tensor("h_tb", [128, 128], BF16))
        trow = st.enter_context(nc.sbuf_tensor("h_trow", [1, 128], BF16))
        tcolb = st.enter_context(nc.sbuf_tensor("h_tcolb", [128, 1], BF16))
        scl = st.enter_context(nc.sbuf_tensor("h_scl", [128, 8], F32))
        oall = st.enter_context(nc.sbuf_tensor("h_oall", [128, T], BF16))
        bar = p.last_barrier
        t_wg2 = k.step("gpsimd", lambda e: e.dma_start(out=wg2[:], in_=p.io["l1_Wg2"]), waits=bar, dma=True)
        t_bg2 = k.step("sync", lambda e: e.dma_start(out=bg2[:], in_=p.io["l1_bg2"]), waits=bar, dma=True)
        t_nw = k.step("sync", lambda e: e.dma_start(out=nwc[:], in_=p.io["l1_normw"]), waits=bar, dma=True)
        phl = [t_wg2, t_bg2, t_nw]

        B0, B1, B2, B3, B4 = [p.ps_f[:, i * 512:(i + 1) * 512] for i in range(5)]
        ldtok = {}
        consumed = {}
        lastXg = None
        lastStore = [None, None]
        lastOallStore = [None]

        def emit_loads(bh):
            b_, h_ = bh // H, bh % H
            sl = bh % 2
            w = list(bar) if bh < 2 else list(consumed[bh - 2])
            toks = []
            for (dst, srcn) in [(qb_, "q1_cm"), (kb_, "k1_cm"), (vb_, "v1_cm"), (wb_, "w1_cm")]:
                toks.append(k.step("sync", lambda e, dst=dst, srcn=srcn, b_=b_, h_=h_, sl=sl: e.dma_start(
                    out=dst[:, sl * T:(sl + 1) * T],
                    in_=p.io[srcn][h_ * 128:(h_ + 1) * 128, b_ * T:(b_ + 1) * T]), waits=w, dma=True))
            toks.append(k.step("sync", lambda e, b_=b_, h_=h_, sl=sl: e.dma_start(
                out=gdb[:, sl * T:(sl + 1) * T],
                in_=p.io["glogdev_cm"][h_ * 128:(h_ + 1) * 128, b_ * T:(b_ + 1) * T]), waits=w, dma=True))
            toks.append(k.step("sync", lambda e, b_=b_, h_=h_, sl=sl: e.dma_start(
                out=brow[:, sl * T:(sl + 1) * T],
                in_=p.io["beta_cm"][h_:h_ + 1, b_ * T:(b_ + 1) * T]), waits=w, dma=True))
            ldtok[bh] = toks

        emit_loads(0); emit_loads(1)
        for bh in range(B * H):
            b_, h_ = bh // H, bh % H
            sl = bh % 2
            ld = ldtok.pop(bh)
            QB = qb_[:, sl * T:(sl + 1) * T]; KB_ = kb_[:, sl * T:(sl + 1) * T]
            VB = vb_[:, sl * T:(sl + 1) * T]; WB = wb_[:, sl * T:(sl + 1) * T]
            GD = gdb[:, sl * T:(sl + 1) * T]; BR = brow[:, sl * T:(sl + 1) * T]
            if h_ == 0:
                lastXg = k.step("sync", lambda e, b_=b_: e.dma_start(
                    out=xgs[:], in_=p.io["xg1_cm"][:, b_ * T:(b_ + 1) * T]),
                    waits=(list(bar) + list(consumed[bh - 1])) if bh >= 1 else list(bar), dma=True)
            prev = list(consumed[bh - 1]) if bh >= 1 else []
            t_lam = k.step("vector", lambda e, GD=GD: e.tensor_tensor_scan(
                lam[:], GD, GD, 0.0, ALU.add, ALU.bypass), waits=ld + prev)
            t_l0a = k.step("vector", lambda e: e.memset(cols[:, 0:1], 0.0), waits=[t_lam])
            t_l0b = k.step("vector", lambda e: e.tensor_copy(cols[:, 1:16], lam[:, 127:2047:128]), waits=[t_lam])
            t_le = k.step("vector", lambda e: e.tensor_copy(cols[:, 16:32], lam[:, 127:2048:128]), waits=[t_lam])
            t_ee0 = k.step("vector", lambda e: e.scalar_tensor_tensor(
                cols[:, 48:64], cols[:, 16:32], ALPHA * C, cols[:, 0:16], ALU.add, ALU.subtract),
                waits=[t_l0a, t_l0b, t_le])
            t_ee = k.step("scalar", lambda e: e.activation(cols[:, 32:48], cols[:, 48:64], AF.Exp), waits=[t_ee0])
            t_dl = k.step("vector", lambda e: e.tensor_tensor(
                tmpf[:].rearrange("p (c t) -> p c t", c=NCH),
                lam[:].rearrange("p (c t) -> p c t", c=NCH),
                cols[:, 0:16].rearrange("p (c o) -> p c o", o=1).broadcast_to([128, NCH, C]),
                ALU.subtract), waits=[t_l0a, t_l0b])
            t_E1 = k.step("scalar", lambda e: e.activation(E1[:], tmpf[:], AF.Exp), waits=[t_dl])
            t_E1i = k.step("scalar", lambda e: e.activation(E1i[:], tmpf[:], AF.Exp, scale=-1.0), waits=[t_dl])
            t_dl2 = k.step("vector", lambda e: e.tensor_tensor(
                tmpf[:].rearrange("p (c t) -> p c t", c=NCH),
                lam[:].rearrange("p (c t) -> p c t", c=NCH),
                cols[:, 16:32].rearrange("p (c o) -> p c o", o=1).broadcast_to([128, NCH, C]),
                ALU.subtract), waits=[t_le, t_E1, t_E1i])
            t_W2e = k.step("scalar", lambda e: e.activation(W2e[:], tmpf[:], AF.Exp, scale=-1.0), waits=[t_dl2])
            t_sq = k.step("vector", lambda e, WB=WB: e.tensor_tensor(sqw[:], WB, WB, ALU.mult), waits=ld)
            rmm = []
            for cq_ in range(4):
                rmm.append(k.step("tensor", lambda e, cq_=cq_: e.matmul(
                    p.ps_f[0:1, cq_ * 512:cq_ * 512 + 512], p.ones[:, 0:1],
                    sqw[:, cq_ * 512:(cq_ + 1) * 512], start=True, stop=True),
                    waits=([t_sq] + prev) if cq_ == 0 else []))
            t_rw = k.step("vector", lambda e: e.tensor_copy(rowf[:], p.ps_f[0:1, 0:2048]), waits=rmm)
            t_rs = k.step("scalar", lambda e: e.activation(rowf[:], rowf[:], AF.Sqrt, bias=p.cc[0:1, 3:4]), waits=[t_rw])
            t_rr = k.step("vector", lambda e: e.reciprocal(rowf[:], rowf[:]), waits=[t_rs])
            t_rb = k.step("vector", lambda e, BR=BR: e.tensor_tensor(rowf[:], rowf[:], BR, ALU.mult), waits=[t_rr])
            t_rc = k.step("vector", lambda e: e.tensor_copy(rowb[:], rowf[:]), waits=[t_rb])
            hwt = []
            for cq_ in range(4):
                bmm = k.step("tensor", lambda e, cq_=cq_: e.matmul(
                    p.ps_f[:, cq_ * 512:(cq_ + 1) * 512], p.ones_row[:],
                    rowb[0:1, cq_ * 512:(cq_ + 1) * 512], start=True, stop=True),
                    waits=[t_rc, t_rw] if cq_ == 0 else [])
                hwt.append(k.step("vector", lambda e, cq_=cq_, WB=WB: e.tensor_tensor(
                    hw[:, cq_ * 512:(cq_ + 1) * 512], WB[:, cq_ * 512:(cq_ + 1) * 512],
                    p.ps_f[:, cq_ * 512:(cq_ + 1) * 512], ALU.mult), waits=[bmm]))
            t_q2 = k.step("vector", lambda e, QB=QB: e.tensor_tensor(q2[:], QB, p.pramp[:], ALU.mult), waits=ld + prev)
            t_W1 = k.step("vector", lambda e: e.tensor_tensor(W1[:], hw[:], E1i[:], ALU.mult), waits=hwt + [t_E1i])
            t_We1 = k.step("vector", lambda e: e.tensor_tensor(Wend[:], hw[:], W2e[:], ALU.mult), waits=hwt + [t_W2e])
            t_We = k.step("vector", lambda e: e.tensor_tensor(
                Wend[:].rearrange("p (c t) -> p c t", c=NCH),
                Wend[:].rearrange("p (c t) -> p c t", c=NCH),
                p.prampend[:].unsqueeze(1).broadcast_to([128, NCH, C]), ALU.mult),
                waits=[t_We1])
            prep_done = [t_E1, t_W1, t_We, t_q2, t_ee]
            tok_sk = None; tok_sv = None; tok_skb = None; tok_svb = None
            tok_trsp = None
            lastB0 = None; lastB1 = None; lastB2 = None; lastB3 = None; lastB4 = None
            lastPG = None; lastZg = None; lastS = None; lastA = None
            lastO2 = None; lastGts = None; lastTb = None; o_toks = []
            lastMD = None; lastM4 = None; lastM5 = None; lastO1 = None
            for c in range(NCH):
                CS = slice(c * C, (c + 1) * C)
                wtr = prep_done + ([tok_trsp] if tok_trsp else [])
                tt1 = k.step("tensor", lambda e, CS=CS: e.transpose(
                    p.ps_t[:, 0 * 128:1 * 128], W1[:, CS], p.ident[:]), waits=wtr + ([lastB4] if lastB4 else []))
                k.step("tensor", lambda e, CS=CS: e.transpose(
                    p.ps_t[:, 1 * 128:2 * 128], Wend[:, CS], p.ident[:]))
                k.step("tensor", lambda e, CS=CS, KB_=KB_: e.transpose(
                    p.ps_t[:, 2 * 128:3 * 128], KB_[:, CS], p.ident[:]))
                tt4 = k.step("tensor", lambda e, CS=CS, VB=VB: e.transpose(
                    p.ps_t[:, 3 * 128:4 * 128], VB[:, CS], p.ident[:]))
                if c > 0:
                    tt5 = k.step("tensor", lambda e: e.transpose(
                        p.ps_t[:, 4 * 128:5 * 128], SkTb[:], p.ident[:]), waits=[tok_skb])
                else:
                    tt5 = tt4
                t_tr = k.step("vector", lambda e: e.tensor_copy(trsp[:], p.ps_t[:, 0:640]), waits=[tt5])
                tok_trsp = t_tr
                W1T = trsp[:, 0:128]; WeT = trsp[:, 128:256]; KT = trsp[:, 256:384]
                VT = trsp[:, 384:512]; SkCM = trsp[:, 512:640]
                m1 = k.step("tensor", lambda e, CS=CS, KB_=KB_, QB=QB: e.matmul(
                    B0[:, 0:128], KB_[:, CS], QB[:, CS], start=True, stop=True),
                    waits=([lastPG] if lastPG else hwt))
                wpg = [m1] + ([lastS] if lastS else [])
                t_pg = k.step("vector", lambda e: e.tensor_tensor(PG[:], B0[:, 0:128], p.gt[:], ALU.mult), waits=wpg)
                lastPG = t_pg
                m2 = k.step("tensor", lambda e, c=c, W1T=W1T: e.matmul(
                    B1[:, 0:128], W1T, PG[:], start=True, stop=(c == 0)),
                    waits=[t_pg, t_tr] + ([lastS] if lastS else hwt))
                if c > 0:
                    m2 = k.step("tensor", lambda e, CS=CS: e.matmul(
                        B1[:, 0:128], SkCM, q2[:, CS], start=False, stop=True), waits=[])
                wse = [m2] + ([lastA] if lastA else [])
                t_s = k.step("vector", lambda e, CS=CS: e.tensor_tensor(
                    ssb[:], B1[:, 0:128], E1[:, CS], ALU.mult), waits=wse)
                lastS = t_s
                t_a = k.step("scalar", lambda e: e.activation(abf[:], ssb[:], AF.Exp),
                             waits=[t_s] + ([lastMD] if lastMD else []))
                t_at = k.step("vector", lambda e, CS=CS: e.tensor_tensor(atl[:], abf[:], E1[:, CS], ALU.mult),
                              waits=[t_a] + ([lastM4] if lastM4 else []))
                t_ap = k.step("vector", lambda e, CS=CS: e.tensor_tensor(
                    atp[:], atl[:], p.pramp[:, CS], ALU.mult), waits=[t_at] + ([lastM5] if lastM5 else []))
                lastA = t_ap
                mD = k.step("tensor", lambda e: e.matmul(
                    p.ps_x[:, 0:1], abf[:], p.ones[:, 0:1], start=True, stop=True),
                    waits=[t_a] + ([lastTb] if lastTb else []))
                lastMD = mD
                m4 = k.step("tensor", lambda e, CS=CS: e.matmul(
                    B2[:, 0:128], W1[:, CS], atl[:], start=True, stop=True),
                    waits=[t_at] + ([lastZg] if lastZg else hwt))
                lastM4 = m4
                t_zg = k.step("vector", lambda e: e.tensor_tensor(Zg[:], B2[:, 0:128], p.gt[:], ALU.mult), waits=[m4])
                if c > 0:
                    m5 = k.step("tensor", lambda e: e.matmul(
                        B3[:, 0:128], Svb[:], atp[:], start=True, stop=False),
                        waits=[t_ap, tok_svb] + ([lastO2] if lastO2 else []))
                    lastM5 = m5
                m6 = k.step("tensor", lambda e, c=c, VT=VT: e.matmul(
                    B3[:, 0:128], VT, Zg[:], start=(c == 0), stop=True),
                    waits=[t_zg, t_tr] + (([lastO2] if lastO2 else hwt) if c == 0 else []))
                m7 = k.step("tensor", lambda e: e.matmul(
                    B4[:, 0:128], WeT, KT, start=True, stop=True),
                    waits=[t_tr] + ([tok_sv] if tok_sv else []))
                if c > 0:
                    u1 = k.step("vector", lambda e, c=c: e.tensor_scalar(
                        SkT[:], SkT[:], cols[:, 32 + c:32 + c + 1], None, ALU.mult), waits=[tok_sk, t_ee])
                    tok_sk = k.step("vector", lambda e: e.tensor_tensor(SkT[:], SkT[:], B4[:, 0:128], ALU.add), waits=[u1, m7])
                else:
                    tok_sk = k.step("vector", lambda e: e.tensor_copy(SkT[:], B4[:, 0:128]), waits=[m7])
                tok_skb = k.step("vector", lambda e: e.tensor_copy(SkTb[:], SkT[:]), waits=[tok_sk])
                m8 = k.step("tensor", lambda e: e.matmul(
                    B4[:, 0:128], WeT, VT, start=True, stop=True), waits=[tok_sk])
                if c > 0:
                    u2 = k.step("vector", lambda e, c=c: e.tensor_scalar(
                        Sv[:], Sv[:], cols[:, 32 + c:32 + c + 1], None, ALU.mult), waits=[tok_sv, t_ee])
                    tok_sv = k.step("vector", lambda e: e.tensor_tensor(Sv[:], Sv[:], B4[:, 0:128], ALU.add), waits=[u2, m8])
                else:
                    tok_sv = k.step("vector", lambda e: e.tensor_copy(Sv[:], B4[:, 0:128]), waits=[m8])
                tok_svb = k.step("vector", lambda e: e.tensor_copy(Svb[:], Sv[:]), waits=[tok_sv])
                t_o2 = k.step("scalar", lambda e: e.activation(o2s[:], B3[:, 0:128], AF.Square), waits=[m6])
                mS = k.step("tensor", lambda e: e.matmul(
                    p.ps_x[:, 2:3], o2s[:], p.ones[:, 1:2], start=True, stop=True), waits=[t_o2])
                mg = k.step("tensor", lambda e, h_=h_, CS=CS: e.matmul(
                    B2[:, 0:128], wg2[:, h_ * 128:(h_ + 1) * 128], xgs[:, CS], start=True, stop=True),
                    waits=[t_zg, lastXg, t_wg2] + ([lastGts] if lastGts else []))
                
                t_gt = k.step("scalar", lambda e, h_=h_: e.activation(
                    gts[:], B2[:, 0:128], AF.Sigmoid, bias=bg2[:, h_:h_ + 1]), waits=[mg, t_bg2])
                lastZg = t_gt
                tD = k.step("vector", lambda e: e.tensor_copy(scl[:, 0:1], p.ps_x[:, 0:1]), waits=[mS])
                tD2 = k.step("vector", lambda e: e.tensor_tensor(scl[:, 1:2], scl[:, 0:1], scl[:, 0:1], ALU.mult), waits=[tD])
                tE = k.step("vector", lambda e: e.scalar_tensor_tensor(
                    scl[:, 2:3], scl[:, 1:2], RMS_EPS, p.ps_x[:, 2:3], ALU.mult, ALU.add), waits=[tD2, mS])
                tSq = k.step("scalar", lambda e: e.activation(scl[:, 3:4], scl[:, 2:3], AF.Sqrt), waits=[tE])
                tRc = k.step("vector", lambda e: e.reciprocal(scl[:, 4:5], scl[:, 3:4]), waits=[tSq])
                tCb = k.step("vector", lambda e: e.tensor_copy(tcolb[:], scl[:, 4:5]), waits=[tRc])
                mT = k.step("tensor", lambda e: e.matmul(
                    p.ps_x[0:1, 128:256], tcolb[:], p.ident[:], start=True, stop=True), waits=[tCb])
                tTr = k.step("vector", lambda e: e.tensor_copy(trow[:], p.ps_x[0:1, 128:256]), waits=[mT])
                mB = k.step("tensor", lambda e: e.matmul(
                    p.ps_x[:, 256:384], p.ones_row[:], trow[:], start=True, stop=True),
                    waits=[tTr] + ([lastTb] if lastTb else []))
                t_tb = k.step("scalar", lambda e: e.activation(tb_s[:], p.ps_x[:, 256:384], AF.Copy),
                              waits=[mB] + ([lastO1] if lastO1 else []))
                lastTb = t_tb
                wo1 = [t_tb, t_nw, t_o2, mS]
                t_o1 = k.step("vector", lambda e: e.scalar_tensor_tensor(
                    o2s[:], B3[:, 0:128], nwc[:, 0:1], tb_s[:], ALU.mult, ALU.mult), waits=wo1)
                lastO2 = t_o1
                lastO1 = t_o1
                wof = [t_o1, t_gt] + ([lastOallStore[0]] if (c == 0 and lastOallStore[0]) else [])
                t_of = k.step("vector", lambda e, CS=CS: e.tensor_tensor(
                    oall[:, CS], o2s[:], gts[:], ALU.mult), waits=wof)
                if "dbgH" in p.io and bh == 0 and c in (0, 1):
                    dv_ = p.io["dbgH"]
                    for di, (nm, tile, tk) in enumerate([
                            ("PG", PG, t_pg), ("ssb", ssb, t_s), ("atl", atl, t_at),
                            ("Zg", Zg, t_zg), ("W1c", None, t_tr), ("E1c", None, t_E1),
                            ("trsp0", None, t_tr), ("tb", tb_s, t_tb)]):
                        if tile is None:
                            if nm == "W1c":
                                ap_ = W1[:, CS]
                            elif nm == "E1c":
                                ap_ = E1[:, CS]
                            else:
                                ap_ = trsp[:, 0:128]
                        else:
                            ap_ = tile[:]
                        k.step("gpsimd", lambda e, ap_=ap_, di=di, c=c: e.dma_start(
                            out=dv_[:, (c * 8 + di) * 128:(c * 8 + di + 1) * 128], in_=ap_),
                            waits=[tk, t_of], dma=True)
                    dgt = k.step("vector", lambda e: e.engine_nop(), waits=[k.token("gpsimd")])
                    lastPG = dgt; lastS = dgt; lastZg = dgt; lastA = dgt
                    lastGts = dgt; lastO2 = dgt; lastO1 = dgt
                lastGts = t_of
                o_toks.append(t_of)
            wst = o_toks + ([lastStore[sl]] if lastStore[sl] else [])
            lastStore[sl] = k.step("sync", lambda e, b_=b_, h_=h_: e.dma_start(
                out=p.io["o1_cm"][h_ * 128:(h_ + 1) * 128, b_ * T:(b_ + 1) * T],
                in_=oall[:]), waits=wst, dma=True)
            lastOallStore[0] = lastStore[sl]
            consumed[bh] = [k.token("tensor"), k.token("vector"), k.token("scalar")]
            if bh + 2 < B * H:
                emit_loads(bh + 2)

PHASES["H"] = phase_H

def phase_I(p):
    nc, k = p.nc, p.kb
    import contextlib
    with contextlib.ExitStack() as st:
        wo = st.enter_context(nc.sbuf_tensor("i_wo", [128, DT * D], BF16))
        oc = st.enter_context(nc.sbuf_tensor("i_oc", [128, 2 * DT * 512], BF16))
        ev = st.enter_context(nc.sbuf_tensor("i_ev", [128, 2 * D], BF16))
        bar = p.last_barrier
        o1, s1 = p.io["o1_cm"], p.io["s1_tm"]
        w_hbm = p.io["l1_Wo"].rearrange("(d p) m -> d p m", p=128)
        wld = [k.step("gpsimd", lambda e, d=d: e.dma_start(
            out=wo[:, d * D:(d + 1) * D], in_=w_hbm[d]), waits=(bar if d == 0 else []), dma=True)
            for d in range(DT)]
        wtok = wld[-1]
        gb = [0]; bank_evac = {}
        lastXc = [None, None]; lastEv = [None, None]
        for i in range(8):
            sl = i % 2
            wxc = [wtok] + ([lastXc[sl]] if lastXc[sl] else (bar if i < 2 else []))
            txc = k.step("sync", lambda e, i=i, sl=sl: e.dma_start(
                out=oc[:].rearrange("p (s d t) -> p s d t", s=2, d=DT)[:, sl],
                in_=o1[:, i * 512:(i + 1) * 512].rearrange("(d p) t -> p d t", p=128)),
                waits=wxc, dma=True)
            for tt in range(4):
                evs = []
                for dc in range(4):
                    b = gb[0] % 5; gb[0] += 1
                    wb = [bank_evac[b]] if b in bank_evac else []
                    mm = None
                    for d in range(DT):
                        mm = k.step("tensor", lambda e, d=d, tt=tt, dc=dc, sl=sl, b=b: e.matmul(
                            p.ps_f[:, b * 512:(b + 1) * 512],
                            oc[:, (sl * DT + d) * 512 + tt * 128:(sl * DT + d) * 512 + (tt + 1) * 128],
                            wo[:, d * D + dc * 512:d * D + (dc + 1) * 512],
                            start=(d == 0), stop=(d == DT - 1)), waits=([txc] + wb) if d == 0 else [])
                    wev = [mm] + ([lastEv[sl]] if (dc == 0 and lastEv[sl]) else [])
                    evt = k.step("vector", lambda e, dc=dc, sl=sl, b=b: e.tensor_copy(
                        ev[:, sl * D + dc * 512: sl * D + (dc + 1) * 512],
                        p.ps_f[:, b * 512:(b + 1) * 512]), waits=wev)
                    bank_evac[b] = evt
                    evs.append(evt)
                lastEv[sl] = k.step("sync", lambda e, i=i, tt=tt, sl=sl: e.dma_start(
                    out=s1[i * 512 + tt * 128: i * 512 + (tt + 1) * 128, :],
                    in_=ev[:, sl * D:(sl + 1) * D]), waits=evs, dma=True)
            lastXc[sl] = k.token("tensor")

PHASES["I"] = phase_I

def phase_W(p):
    nc, k = p.nc, p.kb
    import contextlib
    with contextlib.ExitStack() as st:
        wt = st.enter_context(nc.sbuf_tensor("w_wt", [128, 4 * 512], F32))
        acc = st.enter_context(nc.sbuf_tensor("w_acc", [128, 4], F32))
        gth = st.enter_context(nc.sbuf_tensor("w_gth", [128, 2], F32))
        t1 = st.enter_context(nc.sbuf_tensor("w_t1", [128, 512], F32))
        t2 = st.enter_context(nc.sbuf_tensor("w_t2", [128, 512], F32))
        qv = st.enter_context(nc.sbuf_tensor("w_qv", [128, 2 * 512], BF16))
        tev = st.enter_context(nc.sbuf_tensor("w_tev", [128, 2 * 512], BF16))
        onesf = st.enter_context(nc.sbuf_tensor("w_onesf", [128, 128], F32))
        bar = p.last_barrier
        k.step("vector", lambda e: e.memset(onesf[:], 1.0), waits=bar)
        cfgs = [("fw1", "q1T_w", 44, 4), ("fw2", "q2T_w", 44, 4), ("fw3", "q3T_w", 16, 11)]
        lastWt = [None] * 4
        lastQv = [None, None]
        lastTev = [None, None]
        for wi, (wn, qn, NR, NCC) in enumerate(cfgs):
            src = p.io[wn]
            nel = NR * 128 * NCC * 512
            tz = k.step("vector", lambda e: e.memset(acc[:, 0:1], 0.0),
                        waits=[lastWt[0]] if lastWt[0] else [])
            tiles = [(r, cc) for r in range(NR) for cc in range(NCC)]
            for ti, (r, cc) in enumerate(tiles):
                slw = ti % 4
                wld = [lastWt[slw]] if lastWt[slw] else (list(bar) if wi == 0 else [])
                tl = k.step("gpsimd", lambda e, r=r, cc=cc, slw=slw, src=src: e.dma_start(
                    out=wt[:, slw * 512:(slw + 1) * 512],
                    in_=src[r * 128:(r + 1) * 128, cc * 512:(cc + 1) * 512]), waits=wld, dma=True)
                tr_ = k.step("vector", lambda e, slw=slw: e.reduce_sum(
                    acc[:, 1:2], wt[:, slw * 512:(slw + 1) * 512], axis=AX.X,
                    apply_absolute_value=True), waits=[tl, tz])
                tz = k.step("vector", lambda e: e.tensor_tensor(
                    acc[:, 0:1], acc[:, 0:1], acc[:, 1:2], ALU.add), waits=[tr_])
                lastWt[slw] = tz
            mmt = k.step("tensor", lambda e: e.matmul(
                p.ps_x[:, 0:1], onesf[:], acc[:, 0:1], start=True, stop=True), waits=[tz])
            tg1 = k.step("vector", lambda e, nel=nel: e.tensor_scalar(
                gth[:, 0:1], p.ps_x[:, 0:1], 1.0 / nel, BIT_EPS, ALU.mult, ALU.add), waits=[mmt])
            tg2 = k.step("vector", lambda e: e.tensor_scalar(
                gth[:, 0:1], gth[:, 0:1], 0.5, None, ALU.mult), waits=[tg1])
            tg3 = k.step("vector", lambda e: e.tensor_scalar(
                gth[:, 1:2], gth[:, 0:1], -1.0, None, ALU.mult), waits=[tg2])
            dstv = p.io[qn]
            for ti, (r, cc) in enumerate(tiles):
                slw = ti % 4
                wld = [lastWt[slw]] if lastWt[slw] else []
                tl = k.step("gpsimd", lambda e, r=r, cc=cc, slw=slw, src=src: e.dma_start(
                    out=wt[:, slw * 512:(slw + 1) * 512],
                    in_=src[r * 128:(r + 1) * 128, cc * 512:(cc + 1) * 512]), waits=wld, dma=True)
                tq1 = k.step("vector", lambda e, slw=slw: e.tensor_scalar(
                    t1[:], wt[:, slw * 512:(slw + 1) * 512], gth[:, 0:1], None, ALU.is_gt),
                    waits=[tl, tg3])
                tq2 = k.step("vector", lambda e, slw=slw: e.tensor_scalar(
                    t2[:], wt[:, slw * 512:(slw + 1) * 512], gth[:, 1:2], None, ALU.is_lt), waits=[tl])
                slq = ti % 2
                wq_ = [tq1, tq2] + ([lastQv[slq]] if lastQv[slq] else [])
                tq3 = k.step("vector", lambda e, slq=slq: e.tensor_tensor(
                    qv[:, slq * 512:(slq + 1) * 512], t1[:], t2[:], ALU.subtract), waits=wq_)
                lastWt[slw] = tq3
                trs = []
                wtr = [tq3] + ([lastTev[slq]] if lastTev[slq] else [])
                for v in range(4):
                    trs.append(k.step("tensor", lambda e, v=v, slq=slq: e.transpose(
                        p.ps_t[:, (slq * 4 + v) * 128:(slq * 4 + v + 1) * 128],
                        qv[:, slq * 512 + v * 128: slq * 512 + (v + 1) * 128], p.ident[:]),
                        waits=wtr if v == 0 else []))
                tte = k.step("vector", lambda e, slq=slq: e.tensor_copy(
                    tev[:, slq * 512:(slq + 1) * 512],
                    p.ps_t[:, slq * 512:(slq + 1) * 512]), waits=[trs[-1]])
                lastQv[slq] = tte
                lastTev[slq] = k.step("sync", lambda e, r=r, cc=cc, slq=slq, dstv=dstv: e.dma_start(
                    out=dstv[cc * 512:(cc + 1) * 512, r * 128:(r + 1) * 128]
                        .rearrange("(v p) t -> p v t", p=128),
                    in_=tev[:, slq * 512:(slq + 1) * 512].rearrange("p (v t) -> p v t", v=4)),
                    waits=[tte], dma=True)

PHASES["W"] = phase_W

def phase_J1(p):
    nc, k = p.nc, p.kb
    import contextlib
    NG = 4
    with contextlib.ExitStack() as st:
        w1g = st.enter_context(nc.sbuf_tensor("j_w1", [128, DT * 11 * 128], BF16))
        w2g = st.enter_context(nc.sbuf_tensor("j_w2", [128, DT * 11 * 128], BF16))
        xc = st.enter_context(nc.sbuf_tensor("j_xc", [128, 2 * DT * 512], BF16))
        sg = st.enter_context(nc.sbuf_tensor("j_sg", [128, 2 * 512], F32))
        hb = st.enter_context(nc.sbuf_tensor("j_hb", [128, 4 * 512], BF16))
        bar = p.last_barrier
        xn1, hid = p.io["xn1_cm"], p.io["hid"]
        q1v = p.io["q1T_w"].rearrange("(d p) m -> d p m", p=128)
        q2v = p.io["q2T_w"].rearrange("(d p) m -> d p m", p=128)
        gb = [0]; bank_evac = {}
        lastW = None; lastXc = [None, None]; lastSg = [None, None]; lastHb = [None] * 4
        for g in range(NG):
            wld = []
            for d in range(DT):
                wv_ = ([lastW] if lastW else bar) if d == 0 else []
                wld.append(k.step("gpsimd", lambda e, d=d, g=g: e.dma_start(
                    out=w1g[:, d * 1408:(d + 1) * 1408], in_=q1v[d, :, g * 1408:(g + 1) * 1408]),
                    waits=wv_, dma=True))
                wld.append(k.step("gpsimd", lambda e, d=d, g=g: e.dma_start(
                    out=w2g[:, d * 1408:(d + 1) * 1408], in_=q2v[d, :, g * 1408:(g + 1) * 1408]),
                    waits=[], dma=True))
            wtok = wld[-1]
            for i in range(8):
                sl = i % 2
                wxc = [wtok] + ([lastXc[sl]] if lastXc[sl] else [])
                txc = k.step("sync", lambda e, i=i, sl=sl: e.dma_start(
                    out=xc[:].rearrange("p (s d t) -> p s d t", s=2, d=DT)[:, sl],
                    in_=xn1[:, i * 512:(i + 1) * 512].rearrange("(d p) t -> p d t", p=128)),
                    waits=wxc, dma=True)
                for dh in range(11):
                    bA = gb[0] % 5; gb[0] += 1
                    bB = gb[0] % 5; gb[0] += 1
                    wbA = [bank_evac[bA]] if bA in bank_evac else []
                    wbB = [bank_evac[bB]] if bB in bank_evac else []
                    mmA = None
                    for d in range(DT):
                        mmA = k.step("tensor", lambda e, d=d, dh=dh, sl=sl, b=bA: e.matmul(
                            p.ps_f[:, b * 512:(b + 1) * 512],
                            w1g[:, d * 1408 + dh * 128:d * 1408 + (dh + 1) * 128],
                            xc[:, (sl * DT + d) * 512:(sl * DT + d + 1) * 512],
                            start=(d == 0), stop=(d == DT - 1)), waits=([txc] + wbA) if d == 0 else [])
                    mmB = None
                    for d in range(DT):
                        mmB = k.step("tensor", lambda e, d=d, dh=dh, sl=sl, b=bB: e.matmul(
                            p.ps_f[:, b * 512:(b + 1) * 512],
                            w2g[:, d * 1408 + dh * 128:d * 1408 + (dh + 1) * 128],
                            xc[:, (sl * DT + d) * 512:(sl * DT + d + 1) * 512],
                            start=(d == 0), stop=(d == DT - 1)), waits=wbB if d == 0 else [])
                    ssl = dh % 2
                    wsg = [mmA] + ([lastSg[ssl]] if lastSg[ssl] else [])
                    tsg = k.step("scalar", lambda e, ssl=ssl, b=bA: e.activation(
                        sg[:, ssl * 512:(ssl + 1) * 512], p.ps_f[:, b * 512:(b + 1) * 512],
                        AF.Sigmoid), waits=wsg)
                    bank_evac[bA] = tsg
                    hsl = (i * 11 + dh) % 4
                    wh = [tsg, mmB] + ([lastHb[hsl]] if lastHb[hsl] else [])
                    th = k.step("vector", lambda e, ssl=ssl, hsl=hsl, b=bB: e.tensor_tensor(
                        hb[:, hsl * 512:(hsl + 1) * 512], sg[:, ssl * 512:(ssl + 1) * 512],
                        p.ps_f[:, b * 512:(b + 1) * 512], ALU.mult), waits=wh)
                    bank_evac[bB] = th
                    lastSg[ssl] = th
                    dhg = g * 11 + dh
                    lastHb[hsl] = k.step("sync", lambda e, i=i, dhg=dhg, hsl=hsl: e.dma_start(
                        out=hid[(i * 44 + dhg) * 128:(i * 44 + dhg + 1) * 128, :],
                        in_=hb[:, hsl * 512:(hsl + 1) * 512]), waits=[th], dma=True)
                lastXc[sl] = k.token("tensor")
            lastW = k.token("tensor")

PHASES["J1"] = phase_J1

def phase_J2(p):
    nc, k = p.nc, p.kb
    import contextlib
    with contextlib.ExitStack() as st:
        wd = st.enter_context(nc.sbuf_tensor("j2_wd", [128, 44 * 1024], BF16))
        hs = st.enter_context(nc.sbuf_tensor("j2_hs", [128, 44 * 512], BF16))
        gv = st.enter_context(nc.sbuf_tensor("j2_gv", [128, 2 * 512], BF16))
        xb = st.enter_context(nc.sbuf_tensor("j2_xb", [128, 2 * 512], F32))
        sb = st.enter_context(nc.sbuf_tensor("j2_sb", [128, 2 * 512], BF16))
        tb = st.enter_context(nc.sbuf_tensor("j2_tb", [128, 2 * 512], F32))
        yb = st.enter_context(nc.sbuf_tensor("j2_yb", [128, 2 * 512], F32))
        bar = p.last_barrier
        hid, x1, s1, y = p.io["hid"], p.io["x1_tm"], p.io["s1_tm"], p.io["y"]
        q3v = p.io["q3T_w"].rearrange("(d p) m -> d p m", p=128)
        gb = [0]; bank_evac = {}
        lastHs = None; lastSl = [None, None]
        for half in range(2):
            wld = []
            for d in range(44):
                wv_ = bar if (half == 0 and d == 0) else ([lastHs] if (d == 0 and lastHs) else [])
                wld.append(k.step("gpsimd", lambda e, d=d, half=half: e.dma_start(
                    out=wd[:, d * 1024:(d + 1) * 1024], in_=q3v[d, :, half * 1024:(half + 1) * 1024]),
                    waits=wv_, dma=True))
            wtok = wld[-1]
            for i in range(8):
                hld = []
                for d4 in range(4):
                    wh = [wtok] + ([lastHs] if (d4 == 0 and lastHs) else [])
                    hld.append(k.step("sync", lambda e, i=i, d4=d4: e.dma_start(
                        out=hs[:].rearrange("p (d t) -> p d t", d=44)[:, d4 * 11:(d4 + 1) * 11],
                        in_=hid[(i * 44 + d4 * 11) * 128:(i * 44 + (d4 + 1) * 11) * 128, :]
                        .rearrange("(d p) t -> p d t", p=128)), waits=wh, dma=True))
                htok = hld[-1]
                for tt in range(4):
                    for dc in range(2):
                        b = gb[0] % 5; gb[0] += 1
                        wb = [bank_evac[b]] if b in bank_evac else []
                        mm = None
                        for d in range(44):
                            mm = k.step("tensor", lambda e, d=d, tt=tt, dc=dc, b=b: e.matmul(
                                p.ps_f[:, b * 512:(b + 1) * 512],
                                hs[:, d * 512 + tt * 128:d * 512 + (tt + 1) * 128],
                                wd[:, d * 1024 + dc * 512:d * 1024 + (dc + 1) * 512],
                                start=(d == 0), stop=(d == 43)), waits=([htok] + wb) if d == 0 else [])
                        sl2 = (tt * 2 + dc) % 2
                        row0 = i * 512 + tt * 128
                        col0 = half * 1024 + dc * 512
                        wld2 = [lastSl[sl2]] if lastSl[sl2] else list(bar)
                        tlx = k.step("gpsimd", lambda e, row0=row0, col0=col0, sl2=sl2: e.dma_start(
                            out=xb[:, sl2 * 512:(sl2 + 1) * 512],
                            in_=x1[row0:row0 + 128, col0:col0 + 512]), waits=wld2, dma=True)
                        tls = k.step("sync", lambda e, row0=row0, col0=col0, sl2=sl2: e.dma_start(
                            out=sb[:, sl2 * 512:(sl2 + 1) * 512],
                            in_=s1[row0:row0 + 128, col0:col0 + 512]), waits=wld2, dma=True)
                        wev = [mm] + ([lastSl[sl2]] if lastSl[sl2] else [])
                        evt = k.step("scalar", lambda e, sl2=sl2, b=b: e.activation(
                            gv[:, sl2 * 512:(sl2 + 1) * 512],
                            p.ps_f[:, b * 512:(b + 1) * 512], AF.Silu), waits=wev)
                        bank_evac[b] = evt
                        tv1 = k.step("vector", lambda e, sl2=sl2: e.tensor_tensor(
                            tb[:, sl2 * 512:(sl2 + 1) * 512], gv[:, sl2 * 512:(sl2 + 1) * 512],
                            sb[:, sl2 * 512:(sl2 + 1) * 512], ALU.mult), waits=[evt, tls])
                        tv2 = k.step("vector", lambda e, sl2=sl2: e.tensor_tensor(
                            yb[:, sl2 * 512:(sl2 + 1) * 512], tb[:, sl2 * 512:(sl2 + 1) * 512],
                            xb[:, sl2 * 512:(sl2 + 1) * 512], ALU.add), waits=[tlx, tv1])
                        lastSl[sl2] = k.step("sync", lambda e, row0=row0, col0=col0, sl2=sl2: e.dma_start(
                            out=y[row0:row0 + 128, col0:col0 + 512],
                            in_=yb[:, sl2 * 512:(sl2 + 1) * 512]), waits=[tv2], dma=True)
                lastHs = k.token("tensor")

PHASES["J2"] = phase_J2


ALL_PHASES = ["A", "W", "B", "C", "D", "E", "F1", "F2", "G", "H", "I", "J1", "J2"]

def full_io_spec():
    io = dict(const_io_spec())
    io["x"] = ((NT, D), F32, "ExternalInput")
    for n in ["w_q", "w_k", "w_v", "w_o"]: io[n] = ((2048, 2048), F32, "ExternalInput")
    io["b_qkv"] = ((2, 128, 16), F32, "ExternalInput")
    io["b_v_row"] = ((1, 2048), F32, "ExternalInput")
    io["b_o_row"] = ((1, 2048), F32, "ExternalInput")
    io["w_gate"] = ((2048, DH0), F32, "ExternalInput")
    io["w_up"] = ((2048, DH0), F32, "ExternalInput")
    io["w_down"] = ((DH0, 2048), F32, "ExternalInput")
    for n in ["l1_Wq", "l1_Wk", "l1_Wv", "l1_Ww", "l1_Wo"]: io[n] = ((2048, 2048), F32, "ExternalInput")
    io["l1_conv"] = ((128, 192), F32, "ExternalInput")
    io["l1_Wb"] = ((2048, 16), F32, "ExternalInput")
    io["l1_Wf1"] = ((2048, 128), F32, "ExternalInput")
    io["l1_Wg1"] = ((2048, 128), F32, "ExternalInput")
    io["l1_Wf2"] = ((128, 2048), F32, "ExternalInput")
    io["l1_Wg2"] = ((128, 2048), F32, "ExternalInput")
    io["l1_bg2"] = ((128, 16), F32, "ExternalInput")
    io["l1_normw"] = ((128, 1), F32, "ExternalInput")
    io["fw1"] = ((DH1, 2048), F32, "ExternalInput")
    io["fw2"] = ((DH1, 2048), F32, "ExternalInput")
    io["fw3"] = ((2048, DH1), F32, "ExternalInput")
    for n, sh, dt in [("xn0_cm", (D, NT), BF16), ("q0_cm", (D, NT), BF16), ("k0_cm", (D, NT), BF16),
                      ("v0_htm", (16 * NT, 128), BF16), ("o0_cm", (D, NT), BF16), ("s0_tm", (NT, 2048), BF16),
                      ("hsp", (8 * 64 * 128, 512), BF16), ("g0_tm", (NT, 2048), BF16), ("x1_tm", (NT, 2048), F32),
                      ("xn1_cm", (D, NT), BF16),
                      ("q1_cm", (D, NT), BF16), ("k1_cm", (D, NT), BF16), ("v1_cm", (D, NT), BF16),
                      ("w1_cm", (D, NT), BF16), ("glogdev_cm", (D, NT), F32), ("beta_cm", (16, NT), F32),
                      ("xg1_cm", (128, NT), BF16), ("o1_cm", (D, NT), BF16), ("s1_tm", (NT, 2048), BF16),
                      ("q1T_w", (2048, DH1), BF16), ("q2T_w", (2048, DH1), BF16), ("q3T_w", (DH1, 2048), BF16),
                      ("hid", (8 * 44 * 128, 512), BF16)]:
        io[n] = (sh, dt, "Internal")
    io["y"] = ((NT, 2048), F32, "ExternalOutput")
    return io

_WEIGHT_KEYS = ["w_q", "w_k", "w_v", "w_o", "b_qkv", "b_v_row", "b_o_row",
                "w_gate", "w_up", "w_down",
                "l1_Wq", "l1_Wk", "l1_Wv", "l1_Ww", "l1_Wo", "l1_conv", "l1_Wb",
                "l1_Wf1", "l1_Wg1", "l1_Wf2", "l1_Wg2", "l1_bg2", "l1_normw",
                "fw1", "fw2", "fw3"]

def kernel(**inputs):
    hp = host_prepare(inputs)
    nc = build_program(ALL_PHASES, full_io_spec())
    base = const_in_map(hp)
    for n in _WEIGHT_KEYS:
        base[n] = hp[n]
    in_maps = []
    for ci in range(NCORE):
        m = dict(base)
        m["x"] = hp["_x_shards"][ci]
        in_maps.append(m)
    res = run_bass_kernel_spmd(nc, in_maps, core_ids=list(range(NCORE)))
    out = np.concatenate([np.asarray(r["y"], dtype=np.float32).reshape(B, T, D)
                          for r in res.results], axis=0)
    return out
